# version 13
# baseline (speedup 1.0000x reference)
import contextlib
import numpy as np
import concourse.bass as bass
import concourse.mybir as mybir
from concourse.bass_utils import run_bass_kernel_spmd
from concourse.alu_op_type import AluOpType as OP
from concourse.library_overlay import lower_extended_insts

F32 = mybir.dt.float32
BF16 = mybir.dt.bfloat16
I32 = mybir.dt.int32
AF = mybir.ActivationFunctionType
AX = mybir.AxisListType

D = 1024
TME = 2048
TSEQ = 4096
FI = 2816
DI = 2048
NST = 16
NTILE = TME // 128
TCP = 512
NCH_P = TME // TCP
TCS = 512
NCH_S = TSEQ // TCS
IBLKS = [(0, 512), (512, 512), (1024, 512), (1536, 512), (2048, 512), (2560, 256)]
GROUPS = [[0, 1], [2, 3], [4, 5], [6, 7]]

_REG_DONE = [False]
DEBUG = False


def _register_custom_ops():
    if _REG_DONE[0]:
        return
    import concourse.dve_ops as dve_ops
    from concourse.dve_op import DveOp
    from concourse.dve_spec import Spec, Src0, Src1, C0, C1, C2, Zero, Bin, AluOp, sq, select, lower

    def mk(name, spec, ref):
        sha = lower(spec, ver="v3", perf_en=False, name=name).sha("v3")
        return DveOp(name, spec, subdim=False, uops_sha={"v3": sha}, reference=ref)

    x2 = sq(Src0)
    a_spec = Spec(body=Bin(AluOp.ADD,
                           Bin(AluOp.ADD, Bin(AluOp.IS_GT, x2, C0), Bin(AluOp.IS_GT, x2, C1)),
                           Bin(AluOp.IS_GT, x2, C2)))

    def a_ref(in0, in1, s0, s1, imm2):
        f = np.float64 if in0.dtype == np.float64 else np.float32
        t = (in0.astype(f)) ** 2
        return ((t > s0).astype(f) + (t > s1) + (t > imm2)).astype(f)

    s56 = Bin(AluOp.ADD, Bin(AluOp.IS_GT, x2, C1), Bin(AluOp.IS_GT, x2, C2))
    b_spec = Spec(body=Bin(AluOp.ADD, Src1,
                           Bin(AluOp.ADD, Bin(AluOp.IS_GT, x2, C0), Bin(AluOp.ADD, s56, s56))))

    def b_ref(in0, in1, s0, s1, imm2):
        f = np.float64 if in0.dtype == np.float64 else np.float32
        t = (in0.astype(f)) ** 2
        return (in1.astype(f) + (t > s0) + 2.0 * ((t > s1).astype(f) + (t > imm2))).astype(f)

    t7 = Bin(AluOp.IS_GT, x2, C0)
    d7 = Bin(AluOp.ADD, t7, t7)
    g = Bin(AluOp.ADD, Src1, Bin(AluOp.ADD, d7, d7))
    c_spec = Spec(body=select(Bin(AluOp.IS_LT, Src0, Zero), Bin(AluOp.SUBTRACT, Zero, g), g))

    def c_ref(in0, in1, s0, s1, imm2):
        f = np.float64 if in0.dtype == np.float64 else np.float32
        t = (in0.astype(f)) ** 2
        gg = in1.astype(f) + 4.0 * (t > s0).astype(f)
        return np.where(in0 < 0, -gg, gg).astype(f)

    ops = [mk("FP4A", a_spec, a_ref), mk("FP4B", b_spec, b_ref), mk("FP4C", c_spec, c_ref)]
    replace = ["DVEHASH32L_SEED", "DVEHASH32L_BODY", "DVEHASH32L_DRAIN"]
    for new_op, old_name in zip(ops, replace):
        if any(o.name == new_op.name for o in dve_ops.OPS):
            continue
        idx = next(i for i, o in enumerate(dve_ops.OPS) if o.name == old_name)
        row = dve_ops._SUB_OPCODE_FOR_NAME[old_name]
        dve_ops.OPS[idx] = new_op
        dve_ops._SUB_OPCODE_FOR_NAME[new_op.name] = row
        dve_ops.CUSTOM_DVE_SPECS[new_op.name] = new_op.spec
        dve_ops.CUSTOM_DVE_REFS[new_op.name] = new_op.ref
    _REG_DONE[0] = True


def _get_op(name):
    import concourse.dve_ops as dve_ops
    return next(o for o in dve_ops.OPS if o.name == name)


class Em:

    def __init__(self, nc, stack):
        self.nc = nc
        self.sem = {}
        for nm in ["v", "s", "g", "t", "q", "dq", "ds", "dg"]:
            self.sem[nm] = stack.enter_context(nc.semaphore("sem_" + nm))
        self.cnt = {k: 0 for k in self.sem}
        self.hw = {}
        self.writer = {}
        self.readers = {}
        self.queue = {k: [] for k in ["v", "s", "g", "t", "q"]}

    def _wait(self, e, sem, val, waits):
        if val <= 0:
            return
        key = (e, sem)
        if self.hw.get(key, 0) >= val:
            return
        waits.append((self.sem[sem], val))
        self.hw[key] = val

    def _key(self, buf):
        t = getattr(buf, "tensor", buf)
        return getattr(t, "name", str(t))

    def op(self, e, fn, reads=(), writes=(), dma=False):
        waits = []
        for b in reads:
            for sm, vv in self.writer.get(self._key(b), {}).items():
                if sm == e and e == "t":
                    continue
                self._wait(e, sm, vv, waits)
        for b in writes:
            k = self._key(b)
            for sm, vv in self.writer.get(k, {}).items():
                if sm == e and e == "t":
                    continue
                self._wait(e, sm, vv, waits)
            for (sm, vv) in self.readers.get(k, ()):
                if sm == e and e == "t":
                    continue
                self._wait(e, sm, vv, waits)
        if dma:
            sm = "d" + e
            self.cnt[sm] += 16
            done = (sm, self.cnt[sm])
            inc = (self.sem[sm], 16)
        else:
            self.cnt[e] += 1
            done = (e, self.cnt[e])
            inc = (self.sem[e], 1)
        self.queue[e].append((waits, fn, inc))
        for b in reads:
            self.readers.setdefault(self._key(b), set()).add(done)
        for b in writes:
            k = self._key(b)
            d = self.writer.setdefault(k, {})
            d[done[0]] = done[1]
            self.readers[k] = set()

    def barrier(self):
        tgt = dict(self.cnt)
        for e in ["v", "s", "g", "t", "q"]:
            waits = []
            for sm, val in tgt.items():
                if sm == e:
                    continue
                self._wait(e, sm, val, waits)
            if waits:
                self.queue[e].append((waits, None, None))

    def final_join(self):
        waits = []
        for sm, val in self.cnt.items():
            self._wait("q", sm, val, waits)
        if waits:
            self.queue["q"].append((waits, None, None))

    def replay(self, eng_handle, e):
        for waits, fn, inc in self.queue[e]:
            for semh, val in waits:
                eng_handle.wait_ge(semh, val)
            if fn is not None:
                inst = fn(eng_handle)
                inst.then_inc(inc[0], inc[1])


class Ovl:

    BASE = 28672

    def __init__(self, nc):
        self.nc = nc
        self.off = self.BASE
        self.n = 0

    def reset(self):
        self.off = self.BASE

    def t(self, name, shape, dt):
        per = int(np.prod(shape[1:])) * mybir.dt.size(dt)
        off = (self.off + 31) // 32 * 32
        self.off = off + per
        assert self.off < 225000, f"sbuf overlay overflow at {name}: {self.off}"
        self.n += 1
        return self.nc.alloc_sbuf_tensor_at(f"{name}_{self.n}", shape, dt, offset=off)


def _cc(g, kind, ins, outs):
    return g.collective_compute(kind, mybir.AluOpType.bypass,
                                replica_groups=GROUPS, ins=ins, outs=outs)


def build_program():
    _register_custom_ops()
    op_a, op_b, op_c = _get_op("FP4A"), _get_op("FP4B"), _get_op("FP4C")
    stack = contextlib.ExitStack()
    nc = bass.Bass("TRN2", target_bir_lowering=False)

    din = {}
    for nm, shape in [("x", [TME, D]), ("Win", [D, 3072]), ("bin_t", [128, 24]),
                      ("Wx", [DI, 96]), ("bx_t", [96, 1]), ("Wdt", [64, 1024]),
                      ("bdt_t", [128, 8]), ("alog_t", [128, 128]), ("dp_t", [128, 8]),
                      ("Wout", [DI, D]), ("bout_r", [1, D]),
                      ("w_gate", [FI, D]), ("w_up", [FI, D]), ("w_down", [D, FI])]:
        din[nm] = nc.dram_tensor(nm, shape, F32, kind="ExternalInput").ap()
    din["hsel"] = nc.dram_tensor("hsel", [1, 1], I32, kind="ExternalInput").ap()
    out_hbm = nc.dram_tensor("out", [TME, D], F32, kind="ExternalOutput").ap()

    wqgT = nc.dram_tensor("wqgT", [8, 128, 22, 128], BF16, kind="Internal").ap()
    wquT = nc.dram_tensor("wquT", [8, 128, 22, 128], BF16, kind="Internal").ap()
    wqdT = nc.dram_tensor("wqdT", [22, 128, 8, 128], BF16, kind="Internal").ap()
    win_bf = nc.dram_tensor("win_bf", [D, 3072], BF16, kind="Internal").ap()
    wout_bf = nc.dram_tensor("wout_bf", [DI, D], BF16, kind="Internal").ap()
    xres = nc.dram_tensor("xres", [TME, D], F32, kind="Internal").ap()
    xqt_sp = nc.dram_tensor("xqt_sp", [TME, D], BF16, kind="Internal").ap()
    hqt_sp = nc.dram_tensor("hqt_sp", [TME, FI], BF16, kind="Internal").ap()
    cxl_in = [nc.dram_tensor(f"cxl_in{i}", [8, 128, TME // 2], BF16, kind="Internal").ap()
              for i in range(2)]
    cxl_out = [nc.dram_tensor(f"cxl_out{i}", [2, 8, 128, TME // 2], BF16, kind="Internal").ap()
               for i in range(2)]
    cyg_in = [nc.dram_tensor(f"cyg_in{i}", [8, 128, TSEQ // 2], BF16, kind="Internal").ap()
              for i in range(2)]
    cyg_out = [nc.dram_tensor(f"cyg_out{i}", [2, 8, 128, TSEQ // 2], BF16, kind="Internal").ap()
               for i in range(2)]
    bc_dram = nc.dram_tensor("bc_dram", [32, TCS], BF16, kind="Internal").ap()

    em = Em(nc, stack)
    ovl = Ovl(nc)

    A = lambda name, shape, dt: nc.alloc_sbuf_tensor(name, shape, dt)
    fsk_dr = {F32: nc.dram_tensor("fsk_dr_f", [128, 16], F32, kind="Internal").ap(),
              BF16: nc.dram_tensor("fsk_dr_h", [128, 16], BF16, kind="Internal").ap()}
    fsk_sb = {F32: A("fsk_sb_f", [128, 16], F32),
              BF16: A("fsk_sb_h", [128, 16], BF16)}

    def fence(e, bufs, src_ap, dt, sbuf_src=True):
        sink = fsk_dr[dt] if sbuf_src else fsk_sb[dt].ap()[:, 0:16]
        em.op(e, lambda eng, sink=sink, src_ap=src_ap: eng.dma_start(
            out=sink, in_=src_ap), reads=[], writes=list(bufs), dma=True)
    ident = A("ident", [128, 128], BF16)
    ii32 = A("ii32", [128, 128], I32)
    ones_col = A("ones_col", [128, 1], F32)
    ones_bf = A("ones_bf", [1, 128], BF16)
    rstd_a = A("rstd_a", [128, NTILE], F32)
    nmr_a = A("nmr_a", [128, NTILE], F32)
    sums_a = A("sums_a", [128, NTILE], F32)
    ssq_a = A("ssq_a", [128, NTILE], F32)
    stat_t = A("stat_t", [128, NTILE], F32)
    stat_u = A("stat_u", [128, NTILE], F32)
    bin_sb = A("bin_sb", [128, 24], F32)
    bx_sb = A("bx_sb", [96, 1], F32)
    bdt_sb = A("bdt_sb", [128, 8], F32)
    a_sb = A("a_sb", [128, 128], F32)
    dp_sb = A("dp_sb", [128, 8], F32)
    hsel_sb = A("hsel_sb", [1, 1], I32)
    bout_sb = A("bout_sb", [1, D], F32)
    bout_bf = A("bout_bf", [1, D], BF16)
    state = A("state", [128, 128], F32)

    psp = [stack.enter_context(nc.psum_tensor(f"ps_{i}", [128, 512], F32))
           for i in range(4)]
    pst = [stack.enter_context(nc.psum_tensor(f"pst_{i}", [128, 1024], BF16))
           for i in range(2)]

    def quant_chain(P, Fdim, src_f32, dst_bf16, tmp, pre=None):
        nb = Fdim // 16
        m, r6, xn, p1, p2 = tmp["m"], tmp["r6"], tmp["xn"], tmp["p1"], tmp["p2"]
        q = p1
        src3 = src_f32.rearrange("p (a b) -> p a b", b=16)
        if pre == "g":
            mn = tmp["mn"]
            em.op("g", lambda g: g.tensor_reduce(m[:, 0:nb], src3, AX.X, OP.max),
                  reads=[src_f32], writes=[m])
            em.op("g", lambda g: g.tensor_reduce(mn[:, 0:nb], src3, AX.X, OP.min),
                  reads=[src_f32], writes=[mn])
            em.op("v", lambda v: v.scalar_tensor_tensor(
                m[:, 0:nb], mn[:, 0:nb], -1.0, m[:, 0:nb], OP.mult, OP.max),
                reads=[mn, m], writes=[m])
        else:
            em.op("v", lambda v: v.tensor_reduce(m[:, 0:nb], src3, AX.X, OP.max,
                                                 apply_absolute_value=True),
                  reads=[src_f32], writes=[m])
        em.op("v", lambda v: v.tensor_scalar(m[:, 0:nb], m[:, 0:nb], 6e-12, None, OP.max),
              reads=[m], writes=[m])
        em.op("v", lambda v: v.reciprocal_approx_fast(out=r6[:, 0:nb], in_=m[:, 0:nb]),
              reads=[m], writes=[r6])
        em.op("v", lambda v: v.tensor_scalar(r6[:, 0:nb], r6[:, 0:nb], 6.0, None, OP.mult),
              reads=[r6], writes=[r6])
        r6b = r6[:, 0:nb].unsqueeze(2).broadcast_to([P, nb, 16])
        em.op("v", lambda v: v.tensor_tensor(xn[:, 0:Fdim].rearrange("p (a b) -> p a b", b=16),
                                             src3, r6b, OP.mult),
              reads=[src_f32, r6], writes=[xn])
        em.op("v", lambda v: v._custom_dve(op_a, out=p1[:, 0:Fdim], in0=xn[:, 0:Fdim],
                                           s0=0.0625, s1=0.5625, imm2=1.5625),
              reads=[xn], writes=[p1])
        em.op("v", lambda v: v._custom_dve(op_b, out=p2[:, 0:Fdim], in0=xn[:, 0:Fdim],
                                           in1=p1[:, 0:Fdim], s0=3.0625, s1=6.25, imm2=12.25),
              reads=[xn, p1], writes=[p2])
        em.op("v", lambda v: v._custom_dve(op_c, out=q[:, 0:Fdim], in0=xn[:, 0:Fdim],
                                           in1=p2[:, 0:Fdim], s0=25.0, s1=0.0, imm2=0.0),
              reads=[xn, p2], writes=[p1])
        em.op("v", lambda v: v.tensor_scalar(r6[:, 0:nb], m[:, 0:nb], 1.0 / 12.0, None, OP.mult),
              reads=[m], writes=[r6])
        em.op("v", lambda v: v.tensor_tensor(dst_bf16.rearrange("p (a b) -> p a b", b=16),
                                             q[:, 0:Fdim].rearrange("p (a b) -> p a b", b=16),
                                             r6[:, 0:nb].unsqueeze(2).broadcast_to([P, nb, 16]),
                                             OP.mult),
              reads=[p1, r6], writes=[dst_bf16])

    def ln_stats(src_ap, src_buf, ti, xtmp, xtmp_buf):
        em.op("v", lambda v: v.tensor_reduce(sums_a.ap()[:, ti:ti + 1], src_ap, AX.X, OP.add),
              reads=[src_buf], writes=[sums_a])
        em.op("s", lambda s: s.activation(xtmp, src_ap, AF.Square,
                                          accum_out=ssq_a.ap()[:, ti:ti + 1]),
              reads=[src_buf], writes=[ssq_a, xtmp_buf])

    def ln_finalize():
        em.op("v", lambda v: v.tensor_scalar(stat_t.ap(), sums_a.ap(), 1.0 / D, None, OP.mult),
              reads=[sums_a], writes=[stat_t])
        em.op("v", lambda v: v.tensor_tensor(stat_u.ap(), stat_t.ap(), stat_t.ap(), OP.mult),
              reads=[stat_t], writes=[stat_u])
        em.op("v", lambda v: v.scalar_tensor_tensor(stat_u.ap(), ssq_a.ap(), 1.0 / D,
                                                    stat_u.ap(), OP.mult, OP.subtract),
              reads=[ssq_a, stat_u], writes=[stat_u])
        em.op("v", lambda v: v.tensor_scalar(stat_u.ap(), stat_u.ap(), 1e-5, None, OP.add),
              reads=[stat_u], writes=[stat_u])
        em.op("v", lambda v: v.reciprocal(rstd_a.ap(), stat_u.ap()),
              reads=[stat_u], writes=[rstd_a])
        em.op("s", lambda s: s.activation(rstd_a.ap(), rstd_a.ap(), AF.Sqrt),
              reads=[rstd_a], writes=[rstd_a])
        em.op("v", lambda v: v.tensor_tensor(nmr_a.ap(), rstd_a.ap(), rstd_a.ap(), OP.mult),
              reads=[rstd_a], writes=[nmr_a])
        em.op("v", lambda v: v.tensor_tensor(nmr_a.ap(), nmr_a.ap(), stat_u.ap(), OP.mult),
              reads=[nmr_a, stat_u], writes=[nmr_a])
        em.op("v", lambda v: v.tensor_scalar(nmr_a.ap(), nmr_a.ap(), -0.5, 1.5, OP.mult, OP.add),
              reads=[nmr_a], writes=[nmr_a])
        em.op("v", lambda v: v.tensor_tensor(rstd_a.ap(), rstd_a.ap(), nmr_a.ap(), OP.mult),
              reads=[rstd_a, nmr_a], writes=[rstd_a])
        em.op("v", lambda v: v.scalar_tensor_tensor(nmr_a.ap(), stat_t.ap(), -1.0,
                                                    rstd_a.ap(), OP.mult, OP.mult),
              reads=[stat_t, rstd_a], writes=[nmr_a])

    em.op("g", lambda g: g.iota(ii32.ap(), pattern=[[1, 128]], base=0, channel_multiplier=-1),
          writes=[ii32])
    em.op("v", lambda v: v.tensor_scalar(ident.ap(), ii32.ap(), 0, None, OP.is_equal),
          reads=[ii32], writes=[ident])
    em.op("v", lambda v: v.memset(ones_col.ap(), 1.0), writes=[ones_col])
    em.op("v", lambda v: v.memset(ones_bf.ap(), 1.0), writes=[ones_bf])
    em.op("v", lambda v: v.memset(state.ap(), 0.0), writes=[state])
    for nm, dst in [("bin_t", bin_sb), ("bx_t", bx_sb), ("bdt_t", bdt_sb),
                    ("alog_t", a_sb), ("dp_t", dp_sb), ("bout_r", bout_sb),
                    ("hsel", hsel_sb)]:
        em.op("q", lambda q, nm=nm, dst=dst: q.dma_start(out=dst.ap(), in_=din[nm]),
              writes=[dst], dma=True)
    em.op("s", lambda s: s.activation(a_sb.ap(), a_sb.ap(), AF.Exp),
          reads=[a_sb], writes=[a_sb])
    em.op("v", lambda v: v.tensor_scalar(a_sb.ap(), a_sb.ap(), -1.0, None, OP.mult),
          reads=[a_sb], writes=[a_sb])
    em.op("v", lambda v: v.tensor_copy(bout_bf.ap(), bout_sb.ap()),
          reads=[bout_sb], writes=[bout_bf])

    ovl.reset()
    wload = [ovl.t("wload", [128, 2 * DI], F32) for _ in range(2)]
    wq16 = [ovl.t("wq16", [128, FI], BF16) for _ in range(2)]
    wcast = ovl.t("wcast", [128, 2 * DI], BF16)
    wstage = [ovl.t("wstage", [128, 1024], BF16) for _ in range(2)]
    qm = ovl.t("qm", [128, 256], F32)
    qmn = ovl.t("qmn", [128, 256], F32)
    qr6 = ovl.t("qr6", [128, 256], F32)
    qxn = ovl.t("qxn", [128, FI], F32)
    qp1 = ovl.t("qp1", [128, FI], F32)
    qp2 = ovl.t("qp2", [128, FI], F32)
    tmp = {"m": qm.ap(), "mn": qmn.ap(), "r6": qr6.ap(), "xn": qxn.ap(),
           "p1": qp1.ap(), "p2": qp2.ap()}

    def transpose_blocks_to_hbm(src_sb, n_blk, dst_hbm, dst_rfn, rr=[0]):
        for j0 in range(0, n_blk, 8):
            jn = min(8, n_blk - j0)
            pt = pst[rr[0] % 2]
            ws = wstage[rr[0] % 2]
            rr[0] += 1
            for jj in range(jn):
                j = j0 + jj
                em.op("t", lambda t, j=j, jj=jj, pt=pt: t.transpose(
                    pt.ap()[:, jj * 128:(jj + 1) * 128],
                    src_sb.ap()[:, j * 128:(j + 1) * 128], ident.ap()),
                    reads=[src_sb, ident], writes=[pt])
            em.op("v", lambda v, jn=jn, pt=pt, ws=ws: v.tensor_copy(
                ws.ap()[:, 0:jn * 128], pt.ap()[:, 0:jn * 128]),
                reads=[pt], writes=[ws])
            em.op("s", lambda s, j0=j0, jn=jn, ws=ws: s.dma_start(
                out=dst_rfn(j0, jn),
                in_=ws.ap()[:, 0:jn * 128].rearrange("p (j t) -> p j t", t=128)),
                reads=[ws], writes=[dst_hbm], dma=True)

    wi = [0]
    for wname, wdst in [("w_gate", wqgT), ("w_up", wquT)]:
        for k in range(22):
            p = wi[0] % 2
            wi[0] += 1
            em.op("q", lambda q, wname=wname, k=k, p=p: q.dma_start(
                out=wload[p].ap()[:, 0:D], in_=din[wname][k * 128:(k + 1) * 128, :]),
                writes=[wload[p]], dma=True)
            fence("q", [wload[p]], wload[p].ap()[:, 0:16], F32)
            quant_chain(128, D, wload[p].ap()[:, 0:D], wq16[p].ap()[:, 0:D], tmp)
            transpose_blocks_to_hbm(wq16[p], 8, wdst,
                                    lambda j0, jn, k=k, wdst=wdst:
                                    wdst[j0:j0 + jn, :, k, :].transpose([1, 0, 2]))
    for k in range(8):
        p = wi[0] % 2
        wi[0] += 1
        em.op("q", lambda q, k=k, p=p: q.dma_start(
            out=wload[p].ap()[:, 0:FI], in_=din["w_down"][k * 128:(k + 1) * 128, :]),
            writes=[wload[p]], dma=True)
        fence("q", [wload[p]], wload[p].ap()[:, 0:16], F32)
        quant_chain(128, FI, wload[p].ap()[:, 0:FI], wq16[p].ap()[:, 0:FI], tmp)
        transpose_blocks_to_hbm(wq16[p], 22, wqdT,
                                lambda j0, jn, k=k: wqdT[j0:j0 + jn, :, k, :].transpose([1, 0, 2]))
    for k in range(8):
        em.op("q", lambda q, k=k: q.dma_start(
            out=wload[0].ap()[:, 0:3072], in_=din["Win"][k * 128:(k + 1) * 128, :]),
            writes=[wload[0]], dma=True)
        fence("q", [wload[0]], wload[0].ap()[:, 0:16], F32)
        em.op("v", lambda v: v.tensor_copy(wcast.ap()[:, 0:3072], wload[0].ap()[:, 0:3072]),
              reads=[wload[0]], writes=[wcast])
        em.op("s", lambda s, k=k: s.dma_start(
            out=win_bf[k * 128:(k + 1) * 128, :], in_=wcast.ap()[:, 0:3072]),
            reads=[wcast], writes=[win_bf], dma=True)
    for k in range(16):
        em.op("q", lambda q, k=k: q.dma_start(
            out=wload[0].ap()[:, 0:D], in_=din["Wout"][k * 128:(k + 1) * 128, :]),
            writes=[wload[0]], dma=True)
        fence("q", [wload[0]], wload[0].ap()[:, 0:16], F32)
        em.op("v", lambda v: v.tensor_copy(wcast.ap()[:, 0:D], wload[0].ap()[:, 0:D]),
              reads=[wload[0]], writes=[wcast])
        em.op("s", lambda s, k=k: s.dma_start(
            out=wout_bf[k * 128:(k + 1) * 128, :], in_=wcast.ap()[:, 0:D]),
            reads=[wcast], writes=[wout_bf], dma=True)

    em.barrier()

    def ffn_phase(src_hbm, dst_hbm):
        ovl.reset()
        fx0 = [ovl.t("fx0", [128, D], F32) for _ in range(2)]
        fsq = ovl.t("fsq", [128, D], F32)
        for ti in range(NTILE):
            p = ti % 2
            em.op("q", lambda q, ti=ti, p=p: q.dma_start(
                out=fx0[p].ap(), in_=src_hbm[ti * 128:(ti + 1) * 128, :]),
                reads=[src_hbm], writes=[fx0[p]], dma=True)
            fence("q", [fx0[p]], fx0[p].ap()[:, 0:16], F32)
            ln_stats(fx0[p].ap(), fx0[p], ti, fsq.ap(), fsq)
        ln_finalize()
        em.barrier()
        ovl.reset()
        fx = [ovl.t("fx", [128, D], F32) for _ in range(2)]
        fln = [ovl.t("fln", [128, D], F32) for _ in range(2)]
        fq = [ovl.t("fq", [128, D], BF16) for _ in range(2)]
        fqt = [ovl.t("fqt", [128, 8 * 128], BF16) for _ in range(2)]
        qm = ovl.t("qm", [128, 64], F32)
        qmn = ovl.t("qmn", [128, 64], F32)
        qr6 = ovl.t("qr6", [128, 64], F32)
        qxn = ovl.t("qxn", [128, D], F32)
        qp1 = ovl.t("qp1", [128, D], F32)
        qp2 = ovl.t("qp2", [128, D], F32)
        tmp = {"m": qm.ap(), "mn": qmn.ap(), "r6": qr6.ap(), "xn": qxn.ap(),
               "p1": qp1.ap(), "p2": qp2.ap()}
        for ti in range(NTILE):
            p = ti % 2
            em.op("q", lambda q, ti=ti, p=p: q.dma_start(
                out=fx[p].ap(), in_=src_hbm[ti * 128:(ti + 1) * 128, :]),
                reads=[src_hbm], writes=[fx[p]], dma=True)
            fence("q", [fx[p]], fx[p].ap()[:, 0:16], F32)
            em.op("v", lambda v, ti=ti, p=p: v.tensor_scalar(
                fln[p].ap(), fx[p].ap(), rstd_a.ap()[:, ti:ti + 1],
                nmr_a.ap()[:, ti:ti + 1], OP.mult, OP.add),
                reads=[fx[p], rstd_a, nmr_a], writes=[fln[p]])
            quant_chain(128, D, fln[p].ap(), fq[p].ap(), tmp)
            for k in range(8):
                em.op("t", lambda t, k=k, p=p: t.transpose(
                    pst[p].ap()[:, k * 128:(k + 1) * 128],
                    fq[p].ap()[:, k * 128:(k + 1) * 128], ident.ap()),
                    reads=[fq[p], ident], writes=[pst[p]])
            em.op("v", lambda v, p=p: v.tensor_copy(fqt[p].ap(), pst[p].ap()),
                  reads=[pst[p]], writes=[fqt[p]])
            em.op("s", lambda s, ti=ti, p=p: s.dma_start(
                out=xqt_sp[ti * 128:(ti + 1) * 128, :], in_=fqt[p].ap()),
                reads=[fqt[p]], writes=[xqt_sp], dma=True)
        em.barrier()
        ovl.reset()
        wg_sb = ovl.t("wg_sb", [128, 8 * FI], BF16)
        wu_sb = ovl.t("wu_sb", [128, 8 * FI], BF16)
        xqt_t = [ovl.t("xqt_t", [128, 8 * 128], BF16) for _ in range(2)]
        sg = [ovl.t("sg", [128, 512], F32) for _ in range(2)]
        h_sb = [ovl.t("h_sb", [128, FI], F32) for _ in range(2)]
        hq = [ovl.t("hq", [128, FI], BF16) for _ in range(2)]
        hqt = [ovl.t("hqt", [128, 22 * 128], BF16) for _ in range(2)]
        qm = ovl.t("qm", [128, 176], F32)
        qmn = ovl.t("qmn", [128, 176], F32)
        qr6 = ovl.t("qr6", [128, 176], F32)
        qxn = ovl.t("qxn", [128, FI], F32)
        qp1 = ovl.t("qp1", [128, FI], F32)
        qp2 = ovl.t("qp2", [128, FI], F32)
        tmp = {"m": qm.ap(), "mn": qmn.ap(), "r6": qr6.ap(), "xn": qxn.ap(),
               "p1": qp1.ap(), "p2": qp2.ap()}
        for k in range(8):
            em.op("q", lambda q, k=k: q.dma_start(
                out=wg_sb.ap()[:, k * FI:(k + 1) * FI],
                in_=wqgT[k].rearrange("p a t -> p (a t)")),
                reads=[wqgT], writes=[wg_sb], dma=True)
            em.op("q", lambda q, k=k: q.dma_start(
                out=wu_sb.ap()[:, k * FI:(k + 1) * FI],
                in_=wquT[k].rearrange("p a t -> p (a t)")),
                reads=[wquT], writes=[wu_sb], dma=True)
        fence("q", [wg_sb, wu_sb], wu_sb.ap()[:, 0:16], BF16)
        pending_tr = [None]

        def flush_tr():
            if pending_tr[0] is not None:
                pending_tr[0]()
                pending_tr[0] = None

        for ti in range(NTILE):
            p = ti % 2
            em.op("q", lambda q, ti=ti, p=p: q.dma_start(
                out=xqt_t[p].ap(), in_=xqt_sp[ti * 128:(ti + 1) * 128, :]),
                reads=[xqt_sp], writes=[xqt_t[p]], dma=True)
            fence("q", [xqt_t[p]], xqt_t[p].ap()[:, 0:16], BF16)
            for bi, (o0, wd) in enumerate(IBLKS):
                pg = psp[2 * (bi % 2)]
                pu = psp[2 * (bi % 2) + 1]
                for k in range(8):
                    em.op("t", lambda t, k=k, o0=o0, wd=wd, p=p, pg=pg: t.matmul(
                        pg.ap()[:, 0:wd],
                        xqt_t[p].ap()[:, k * 128:(k + 1) * 128],
                        wg_sb.ap()[:, k * FI + o0:k * FI + o0 + wd],
                        start=(k == 0), stop=(k == 7)),
                        reads=[xqt_t[p], wg_sb], writes=[pg])
                for k in range(8):
                    em.op("t", lambda t, k=k, o0=o0, wd=wd, p=p, pu=pu: t.matmul(
                        pu.ap()[:, 0:wd],
                        xqt_t[p].ap()[:, k * 128:(k + 1) * 128],
                        wu_sb.ap()[:, k * FI + o0:k * FI + o0 + wd],
                        start=(k == 0), stop=(k == 7)),
                        reads=[xqt_t[p], wu_sb], writes=[pu])
                em.op("s", lambda s, wd=wd, p=p, pg=pg: s.activation(
                    sg[p].ap()[:, 0:wd], pg.ap()[:, 0:wd], AF.Silu),
                    reads=[pg], writes=[sg[p]])
                em.op("v", lambda v, o0=o0, wd=wd, p=p, pu=pu: v.tensor_tensor(
                    h_sb[p].ap()[:, o0:o0 + wd], sg[p].ap()[:, 0:wd],
                    pu.ap()[:, 0:wd], OP.mult),
                    reads=[sg[p], pu], writes=[h_sb[p]])
            quant_chain(128, FI, h_sb[p].ap(), hq[p].ap(), tmp)
            flush_tr()

            def do_tr(ti=ti, p=p):
                for j0 in range(0, 22, 8):
                    jn = min(8, 22 - j0)
                    pt = pst[(j0 // 8) % 2]
                    for jj in range(jn):
                        j = j0 + jj
                        em.op("t", lambda t, j=j, jj=jj, p=p, pt=pt: t.transpose(
                            pt.ap()[:, jj * 128:(jj + 1) * 128],
                            hq[p].ap()[:, j * 128:(j + 1) * 128], ident.ap()),
                            reads=[hq[p], ident], writes=[pt])
                    em.op("v", lambda v, j0=j0, jn=jn, p=p, pt=pt: v.tensor_copy(
                        hqt[p].ap()[:, j0 * 128:(j0 + jn) * 128], pt.ap()[:, 0:jn * 128]),
                        reads=[pt], writes=[hqt[p]])
                em.op("s", lambda s, ti=ti, p=p: s.dma_start(
                    out=hqt_sp[ti * 128:(ti + 1) * 128, :], in_=hqt[p].ap()),
                    reads=[hqt[p]], writes=[hqt_sp], dma=True)

            pending_tr[0] = do_tr
        flush_tr()
        em.barrier()
        ovl.reset()
        wd_sb = ovl.t("wd_sb", [128, 22 * D], BF16)
        hqt2 = [ovl.t("hqt2", [128, 22 * 128], BF16) for _ in range(2)]
        fx2 = [ovl.t("fx2", [128, D], F32) for _ in range(2)]
        ot = [ovl.t("ot", [128, D], F32) for _ in range(2)]
        for k in range(22):
            em.op("q", lambda q, k=k: q.dma_start(
                out=wd_sb.ap()[:, k * D:(k + 1) * D],
                in_=wqdT[k].rearrange("p a t -> p (a t)")),
                reads=[wqdT], writes=[wd_sb], dma=True)
        fence("q", [wd_sb], wd_sb.ap()[:, 0:16], BF16)
        for ti in range(NTILE):
            p = ti % 2
            em.op("q", lambda q, ti=ti, p=p: q.dma_start(
                out=hqt2[p].ap(), in_=hqt_sp[ti * 128:(ti + 1) * 128, :]),
                reads=[hqt_sp], writes=[hqt2[p]], dma=True)
            em.op("q", lambda q, ti=ti, p=p: q.dma_start(
                out=fx2[p].ap(), in_=src_hbm[ti * 128:(ti + 1) * 128, :]),
                reads=[src_hbm], writes=[fx2[p]], dma=True)
            fence("q", [hqt2[p], fx2[p]], fx2[p].ap()[:, 0:16], F32)
            for half in range(2):
                pd = psp[2 * p + half]
                for k in range(22):
                    em.op("t", lambda t, k=k, half=half, p=p, pd=pd: t.matmul(
                        pd.ap(),
                        hqt2[p].ap()[:, k * 128:(k + 1) * 128],
                        wd_sb.ap()[:, k * D + half * 512:k * D + (half + 1) * 512],
                        start=(k == 0), stop=(k == 21)),
                        reads=[hqt2[p], wd_sb], writes=[pd])
            for half in range(2):
                pd = psp[2 * p + half]
                em.op("v", lambda v, half=half, p=p, pd=pd: v.scalar_tensor_tensor(
                    ot[p].ap()[:, half * 512:(half + 1) * 512], pd.ap(), 0.5,
                    fx2[p].ap()[:, half * 512:(half + 1) * 512], OP.mult, OP.add),
                    reads=[pd, fx2[p]], writes=[ot[p]])
            em.op("s", lambda s, ti=ti, p=p: s.dma_start(
                out=dst_hbm[ti * 128:(ti + 1) * 128, :], in_=ot[p].ap()),
                reads=[ot[p]], writes=[dst_hbm], dma=True)
        fence("s", [dst_hbm], dst_hbm[0:128, 0:16], F32, sbuf_src=False)
        em.barrier()

    ffn_phase(din["x"], xres)

    ovl.reset()
    fx0 = [ovl.t("fx0", [128, D], F32) for _ in range(2)]
    fsq0 = ovl.t("fsq0", [128, D], F32)
    for ti in range(NTILE):
        p = ti % 2
        em.op("q", lambda q, ti=ti, p=p: q.dma_start(
            out=fx0[p].ap(), in_=xres[ti * 128:(ti + 1) * 128, :]),
            reads=[xres], writes=[fx0[p]], dma=True)
        fence("q", [fx0[p]], fx0[p].ap()[:, 0:16], F32)
        ln_stats(fx0[p].ap(), fx0[p], ti, fsq0.ap(), fsq0)
    ln_finalize()
    em.barrier()

    ovl.reset()
    res_c = [ovl.t("res_c", [128, D], F32) for _ in range(2)]
    xln_f = [ovl.t("xln_f", [128, D], BF16) for _ in range(2)]
    xlt_st = [ovl.t("xlt_st", [128, 8 * 128], BF16) for _ in range(2)]
    for ti in range(NTILE):
        p = ti % 2
        em.op("q", lambda q, ti=ti, p=p: q.dma_start(
            out=res_c[p].ap(), in_=xres[ti * 128:(ti + 1) * 128, :]),
            reads=[xres], writes=[res_c[p]], dma=True)
        fence("q", [res_c[p]], res_c[p].ap()[:, 0:16], F32)
        em.op("v", lambda v, ti=ti, p=p: v.tensor_scalar(
            xln_f[p].ap(), res_c[p].ap(),
            rstd_a.ap()[:, ti:ti + 1], nmr_a.ap()[:, ti:ti + 1], OP.mult, OP.add),
            reads=[res_c[p], rstd_a, nmr_a], writes=[xln_f[p]])
        for k in range(8):
            em.op("t", lambda t, k=k, p=p: t.transpose(
                pst[p].ap()[:, k * 128:(k + 1) * 128],
                xln_f[p].ap()[:, k * 128:(k + 1) * 128], ident.ap()),
                reads=[xln_f[p], ident], writes=[pst[p]])
        em.op("v", lambda v, p=p: v.tensor_copy(xlt_st[p].ap(), pst[p].ap()),
              reads=[pst[p]], writes=[xlt_st[p]])
        half = 0 if ti < 8 else 1
        tci = (ti % 8) * 128
        em.op("s", lambda s, ti=ti, p=p, half=half, tci=tci: s.dma_start(
            out=cxl_in[half].transpose([1, 0, 2])[:, :, tci:tci + 128],
            in_=xlt_st[p].ap().rearrange("p (k t) -> p k t", t=128)),
            reads=[xlt_st[p]], writes=[cxl_in[half]], dma=True)
        if ti == 7:
            fence("s", [cxl_in[0]], cxl_in[0][0, 0:128, 0:16], BF16, sbuf_src=False)
            em.op("g", lambda g: _cc(g, "AllGather", [cxl_in[0]], [cxl_out[0]]),
                  reads=[cxl_in[0]], writes=[cxl_out[0]])
    fence("s", [cxl_in[1]], cxl_in[1][0, 0:128, 0:16], BF16, sbuf_src=False)
    em.op("g", lambda g: _cc(g, "AllGather", [cxl_in[1]], [cxl_out[1]]),
          reads=[cxl_in[1]], writes=[cxl_out[1]])
    em.barrier()

    ovl.reset()
    win_sb = ovl.t("win_sb", [128, 8 * 3072], BF16)
    wx_sb = ovl.t("wx_sb", [128, 16 * 96], BF16)
    wdt_sb = ovl.t("wdt_sb", [64, 1024], BF16)
    wldf = ovl.t("wldf", [128, 16 * 96], F32)
    xlnt = [ovl.t("xlnt", [128, 8 * TCS], BF16)] * 2
    xs_c = ovl.t("xs_c", [128, 16 * TCS], BF16)
    z_c = ovl.t("z_c", [128, 8 * TCS], BF16)
    xp_sb = ovl.t("xp_sb", [128, TCS], F32)
    bc16 = ovl.t("bc16", [128, TCS], BF16)
    bcb = ovl.t("bcb", [128, 16 * TCS], BF16)
    bcc = ovl.t("bcc", [128, 16 * TCS], BF16)
    delta_t = [ovl.t("delta_t", [128, TCS], F32) for _ in range(2)]
    sp_t1 = [ovl.t("sp_t1", [128, TCS], F32) for _ in range(2)]
    sp_t2 = [ovl.t("sp_t2", [128, TCS], F32) for _ in range(2)]
    u_t = [ovl.t("u_t", [128, TCS], BF16) for _ in range(2)]
    da2 = [ovl.t(f"da2_{i}", [128, TCS], F32) for i in range(2)]
    w2 = [ovl.t(f"w2_{i}", [128, TCS], BF16) for i in range(2)]
    h2 = [ovl.t(f"h2_{i}", [128, TCS], F32) for i in range(2)]
    hp_all = [ovl.t("hp_all", [128, 16 * TCS], BF16) for _ in range(2)]
    y_s = ovl.t("y_s", [128, TCS], F32)
    yg_t = ovl.t("yg_t", [128, TCS], F32)
    yg2_t = [ovl.t("yg2_t", [128, TCS], BF16) for _ in range(2)]
    em.op("q", lambda q: q.dma_start(
        out=wldf.ap()[:, 0:16 * 96].rearrange("p (k r) -> p k r", r=96),
        in_=din["Wx"].rearrange("(k p) r -> p k r", p=128)),
        writes=[wldf], dma=True)
    fence("q", [wldf], wldf.ap()[:, 0:16], F32)
    em.op("v", lambda v: v.tensor_copy(wx_sb.ap(), wldf.ap()[:, 0:16 * 96]),
          reads=[wldf], writes=[wx_sb])
    em.op("q", lambda q: q.dma_start(out=wldf.ap()[0:64, 0:1024], in_=din["Wdt"]),
          writes=[wldf], dma=True)
    fence("q", [wldf], wldf.ap()[:, 0:16], F32)
    em.op("v", lambda v: v.tensor_copy(wdt_sb.ap(), wldf.ap()[0:64, 0:1024]),
          reads=[wldf], writes=[wdt_sb])
    for k in range(8):
        em.op("q", lambda q, k=k: q.dma_start(
            out=win_sb.ap()[:, k * 3072:(k + 1) * 3072],
            in_=win_bf[k * 128:(k + 1) * 128, :]),
            reads=[win_bf], writes=[win_sb], dma=True)
    fence("q", [win_sb], win_sb.ap()[:, 0:16], BF16)

    pend_tree = [None]
    for ci in range(NCH_S):
        slot = ci // (NCH_S // 2)
        toff = (ci % (NCH_S // 2)) * TCS
        gt0 = ci * TCS
        cp = ci % 2
        xh = 0 if toff < TME // 2 else 1
        toff2 = toff % (TME // 2)
        em.op("q", lambda q, slot=slot, toff2=toff2, cp=cp, xh=xh: q.dma_start(
            out=xlnt[cp].ap().rearrange("p (k t) -> p k t", t=TCS),
            in_=cxl_out[xh][slot, :, :, toff2:toff2 + TCS].transpose([1, 0, 2])),
            reads=[cxl_out[xh]], writes=[xlnt[cp]], dma=True)
        fence("q", [xlnt[cp]], xlnt[cp].ap()[:, 0:16], BF16)
        for fc in range(24):
            pw = psp[fc % 4]
            for k in range(8):
                em.op("t", lambda t, k=k, fc=fc, cp=cp, pw=pw: t.matmul(
                    pw.ap()[:, 0:TCS],
                    win_sb.ap().rearrange("p (k f) -> p k f", f=3072)[:, k, fc * 128:(fc + 1) * 128],
                    xlnt[cp].ap()[:, k * TCS:(k + 1) * TCS],
                    start=(k == 0), stop=(k == 7)),
                    reads=[win_sb, xlnt[cp]], writes=[pw])
            if fc < 16:
                em.op("s", lambda s, fc=fc, pw=pw: s.activation(
                    xs_c.ap()[:, fc * TCS:(fc + 1) * TCS], pw.ap()[:, 0:TCS], AF.Silu,
                    bias=bin_sb.ap()[:, fc:fc + 1]),
                    reads=[pw, bin_sb], writes=[xs_c])
            else:
                em.op("s", lambda s, fc=fc, pw=pw: s.activation(
                    z_c.ap()[:, (fc - 16) * TCS:(fc - 15) * TCS], pw.ap()[:, 0:TCS], AF.Silu,
                    bias=bin_sb.ap()[:, fc:fc + 1]),
                    reads=[pw, bin_sb], writes=[z_c])
        for k in range(16):
            em.op("t", lambda t, k=k: t.matmul(
                psp[0].ap()[0:96, 0:TCS],
                wx_sb.ap()[:, k * 96:(k + 1) * 96],
                xs_c.ap()[:, k * TCS:(k + 1) * TCS],
                start=(k == 0), stop=(k == 15)),
                reads=[wx_sb, xs_c], writes=[psp[0]])
        em.op("v", lambda v: v.tensor_scalar(xp_sb.ap()[0:96, 0:TCS], psp[0].ap()[0:96, 0:TCS],
                                             bx_sb.ap(), None, OP.add),
              reads=[psp[0], bx_sb], writes=[xp_sb])
        em.op("v", lambda v: v.tensor_copy(bc16.ap()[0:96, 0:TCS], xp_sb.ap()[0:96, 0:TCS]),
              reads=[xp_sb], writes=[bc16])
        em.op("g", lambda g: g.dma_start(out=bc_dram, in_=bc16.ap()[64:96, 0:TCS]),
              reads=[bc16], writes=[bc_dram], dma=True)
        em.op("g", lambda g: g.dma_start(
            out=bcb.ap().rearrange("p (n t) -> p n t", t=TCS),
            in_=bc_dram[0:16, :].unsqueeze(0).broadcast_to([128, 16, TCS])),
            reads=[bc_dram], writes=[bcb], dma=True)
        em.op("g", lambda g: g.dma_start(
            out=bcc.ap().rearrange("p (n t) -> p n t", t=TCS),
            in_=bc_dram[16:32, :].unsqueeze(0).broadcast_to([128, 16, TCS])),
            reads=[bc_dram], writes=[bcc], dma=True)
        fence("g", [bcb, bcc], bcc.ap()[:, 0:16], BF16)
        def delta_chain_a(dcp):
            dp_ = dcp % 2
            pw = psp[1 + (dcp % 3)]
            em.op("t", lambda t, dcp=dcp, pw=pw: t.matmul(
                pw.ap()[:, 0:TCS],
                wdt_sb.ap()[:, dcp * 128:(dcp + 1) * 128],
                bc16.ap()[0:64, 0:TCS],
                start=True, stop=True),
                reads=[wdt_sb, bc16], writes=[pw])
            em.op("v", lambda v, dcp=dcp, dp_=dp_, pw=pw: v.tensor_scalar(
                sp_t1[dp_].ap(), pw.ap()[:, 0:TCS], bdt_sb.ap()[:, dcp:dcp + 1], None, OP.add),
                reads=[pw, bdt_sb], writes=[sp_t1[dp_]])
            em.op("v", lambda v, dp_=dp_: v.scalar_tensor_tensor(
                sp_t2[dp_].ap(), sp_t1[dp_].ap(), -1.0, sp_t1[dp_].ap(), OP.mult, OP.min),
                reads=[sp_t1[dp_]], writes=[sp_t2[dp_]])
            em.op("s", lambda s, dp_=dp_: s.activation(sp_t2[dp_].ap(), sp_t2[dp_].ap(), AF.Exp),
                  reads=[sp_t2[dp_]], writes=[sp_t2[dp_]])
            em.op("s", lambda s, dp_=dp_: s.activation(sp_t2[dp_].ap(), sp_t2[dp_].ap(), AF.Ln,
                                                       bias=ones_col.ap()),
                  reads=[sp_t2[dp_], ones_col], writes=[sp_t2[dp_]])

        def delta_chain_b(dcp):
            dp_ = dcp % 2
            em.op("v", lambda v, dp_=dp_: v.scalar_tensor_tensor(
                delta_t[dp_].ap(), sp_t1[dp_].ap(), 0.0, sp_t2[dp_].ap(), OP.max, OP.add),
                reads=[sp_t1[dp_], sp_t2[dp_]], writes=[delta_t[dp_]])
            em.op("v", lambda v, dcp=dcp, dp_=dp_: v.tensor_tensor(
                u_t[dp_].ap(), delta_t[dp_].ap(), xs_c.ap()[:, dcp * TCS:(dcp + 1) * TCS],
                OP.mult),
                reads=[delta_t[dp_], xs_c], writes=[u_t[dp_]])

        delta_chain_a(0)
        delta_chain_b(0)
        for dcp in range(8):
            dp_ = dcp % 2
            for n in range(NST):
                pp = n % 2
                col = dcp * 16 + n
                em.op("s", lambda s, col=col, pp=pp, dp_=dp_: s.activation(
                    da2[pp].ap(), delta_t[dp_].ap(), AF.Exp,
                    scale=a_sb.ap()[:, col:col + 1]),
                    reads=[delta_t[dp_], a_sb], writes=[da2[pp]])
                em.op("v", lambda v, n=n, pp=pp, dp_=dp_: v.tensor_tensor(
                    w2[pp].ap(), u_t[dp_].ap(),
                    bcb.ap()[:, n * TCS:(n + 1) * TCS], OP.mult),
                    reads=[u_t[dp_], bcb], writes=[w2[pp]])
                em.op("v", lambda v, col=col, pp=pp: v.tensor_tensor_scan(
                    h2[pp].ap(), da2[pp].ap(), w2[pp].ap(), state.ap()[:, col:col + 1],
                    OP.mult, OP.add),
                    reads=[da2[pp], w2[pp], state], writes=[h2[pp]])
                em.op("g", lambda g, col=col, pp=pp: g.tensor_copy(
                    state.ap()[:, col:col + 1], h2[pp].ap()[:, TCS - 1:TCS]),
                    reads=[h2[pp]], writes=[state])
                em.op("g", lambda g, n=n, pp=pp, dp_=dp_: g.tensor_tensor(
                    hp_all[dp_].ap()[:, n * TCS:(n + 1) * TCS], h2[pp].ap(),
                    bcc.ap()[:, n * TCS:(n + 1) * TCS], OP.mult),
                    reads=[h2[pp], bcc], writes=[hp_all[dp_]])
                if n == 0 and dcp + 1 < 8:
                    delta_chain_a(dcp + 1)
                if n == 2 and dcp + 1 < 8:
                    delta_chain_b(dcp + 1)
                if n == 4 and pend_tree[0] is not None:
                    pend_tree[0]()
                    pend_tree[0] = None
            def tree_yg(dcp=dcp, dp_=dp_, ci=ci, gt0=gt0):
                hp = hp_all[dp_]
                em.op("v", lambda v, hp=hp: v.tensor_tensor(
                    hp.ap()[:, 0:8 * TCS], hp.ap()[:, 0:8 * TCS], hp.ap()[:, 8 * TCS:16 * TCS],
                    OP.add),
                    reads=[hp], writes=[hp])
                em.op("v", lambda v, hp=hp: v.tensor_tensor(
                    hp.ap()[:, 0:4 * TCS], hp.ap()[:, 0:4 * TCS], hp.ap()[:, 4 * TCS:8 * TCS],
                    OP.add),
                    reads=[hp], writes=[hp])
                em.op("v", lambda v, hp=hp: v.tensor_tensor(
                    hp.ap()[:, 0:2 * TCS], hp.ap()[:, 0:2 * TCS], hp.ap()[:, 2 * TCS:4 * TCS],
                    OP.add),
                    reads=[hp], writes=[hp])
                em.op("v", lambda v, hp=hp: v.tensor_tensor(
                    y_s.ap(), hp.ap()[:, 0:TCS], hp.ap()[:, TCS:2 * TCS], OP.add),
                    reads=[hp], writes=[y_s])
                em.op("v", lambda v, dcp=dcp: v.scalar_tensor_tensor(
                    yg_t.ap(), xs_c.ap()[:, dcp * TCS:(dcp + 1) * TCS],
                    dp_sb.ap()[:, dcp:dcp + 1], y_s.ap(), OP.mult, OP.add),
                    reads=[xs_c, dp_sb, y_s], writes=[yg_t])
                em.op("v", lambda v, dcp=dcp, dp_=dp_: v.tensor_tensor(
                    yg2_t[dp_].ap(), yg_t.ap(), z_c.ap()[:, dcp * TCS:(dcp + 1) * TCS], OP.mult),
                    reads=[yg_t, z_c], writes=[yg2_t[dp_]])
                yh = 0 if ci < NCH_S // 2 else 1
                ygt0 = gt0 % (TSEQ // 2)
                em.op("s", lambda s, dcp=dcp, ygt0=ygt0, dp_=dp_, yh=yh: s.dma_start(
                    out=cyg_in[yh][dcp, :, ygt0:ygt0 + TCS], in_=yg2_t[dp_].ap()),
                    reads=[yg2_t[dp_]], writes=[cyg_in[yh]], dma=True)

            pend_tree[0] = tree_yg
        if pend_tree[0] is not None:
            pend_tree[0]()
            pend_tree[0] = None
        if ci == NCH_S // 2 - 1:
            fence("s", [cyg_in[0]], cyg_in[0][0, 0:128, 0:16], BF16, sbuf_src=False)
            em.op("g", lambda g: _cc(g, "AllGather", [cyg_in[0]], [cyg_out[0]]),
                  reads=[cyg_in[0]], writes=[cyg_out[0]])
    fence("s", [cyg_in[1]], cyg_in[1][0, 0:128, 0:16], BF16, sbuf_src=False)
    em.op("g", lambda g: _cc(g, "AllGather", [cyg_in[1]], [cyg_out[1]]),
          reads=[cyg_in[1]], writes=[cyg_out[1]])
    em.barrier()

    ovl.reset()
    wout_sb = ovl.t("wout_sb", [128, 16 * D], BF16)
    yg2_lo = [ovl.t("yg2_lo", [128, 16 * TCP], BF16) for _ in range(2)]
    yg2_hi = [ovl.t("yg2_hi", [128, 16 * TCP], BF16) for _ in range(2)]
    yg2_e = [ovl.t("yg2_e", [128, 16 * TCP], BF16) for _ in range(2)]
    ymask = ovl.t("ymask", [128, 16 * TCP], mybir.dt.uint8)
    hcol = ovl.t("hcol", [128, 1], I32)
    res_e = [ovl.t("res_e", [128, (TCP // 128) * D], F32) for _ in range(2)]
    sqs = ovl.t("sqs", [128, D], F32)
    em.op("q", lambda q: q.dma_start(out=hcol.ap(), in_=din["hsel"].broadcast_to([128, 1])),
          writes=[hcol], dma=True)
    fence("q", [hcol], fsk_dr[F32][:, 0:16], F32, sbuf_src=False)
    em.op("v", lambda v: v.tensor_copy(ymask.ap(),
                                       hcol.ap().broadcast_to([128, 16 * TCP])),
          reads=[hcol], writes=[ymask])
    for k in range(16):
        em.op("q", lambda q, k=k: q.dma_start(
            out=wout_sb.ap()[:, k * D:(k + 1) * D],
            in_=wout_bf[k * 128:(k + 1) * 128, :]),
            reads=[wout_bf], writes=[wout_sb], dma=True)
    fence("q", [wout_sb], wout_sb.ap()[:, 0:16], BF16)
    for ci in range(NCH_P):
        t0 = ci * TCP
        ntt = TCP // 128
        cp = ci % 2
        em.op("q", lambda q, t0=t0, cp=cp: q.dma_start(
            out=yg2_lo[cp].ap().rearrange("p (k t) -> p k t", t=TCP),
            in_=cyg_out[0].rearrange("j k p t -> p (j k) t")[:, :, t0:t0 + TCP]),
            reads=[cyg_out[0]], writes=[yg2_lo[cp]], dma=True)
        em.op("q", lambda q, t0=t0, cp=cp: q.dma_start(
            out=yg2_hi[cp].ap().rearrange("p (k t) -> p k t", t=TCP),
            in_=cyg_out[1].rearrange("j k p t -> p (j k) t")[:, :, t0:t0 + TCP]),
            reads=[cyg_out[1]], writes=[yg2_hi[cp]], dma=True)
        fence("q", [yg2_lo[cp], yg2_hi[cp]], yg2_hi[cp].ap()[:, 0:16], BF16)
        em.op("v", lambda v, cp=cp: v.select(yg2_e[cp].ap(), ymask.ap(),
                                             yg2_hi[cp].ap(), yg2_lo[cp].ap()),
              reads=[ymask, yg2_hi[cp], yg2_lo[cp]], writes=[yg2_e[cp]])
        for tt in range(ntt):
            ti = (t0 // 128) + tt
            tp2 = tt % 2
            em.op("q", lambda q, ti=ti, tt=tt, cp=cp: q.dma_start(
                out=res_e[cp].ap()[:, tt * D:(tt + 1) * D],
                in_=xres[ti * 128:(ti + 1) * 128, :]),
                reads=[xres], writes=[res_e[cp]], dma=True)
            fence("q", [res_e[cp]], res_e[cp].ap()[:, tt * D:tt * D + 16], F32)
            for half in range(2):
                pd = psp[2 * tp2 + half]
                for k in range(16):
                    em.op("t", lambda t, k=k, half=half, tt=tt, cp=cp, pd=pd: t.matmul(
                        pd.ap(),
                        yg2_e[cp].ap()[:, k * TCP + tt * 128:k * TCP + (tt + 1) * 128],
                        wout_sb.ap()[:, k * D + half * 512:k * D + (half + 1) * 512],
                        start=(k == 0), stop=False),
                        reads=[yg2_e[cp], wout_sb], writes=[pd])
                em.op("t", lambda t, half=half, pd=pd: t.matmul(
                    pd.ap(),
                    ones_bf.ap(),
                    bout_bf.ap()[:, half * 512:(half + 1) * 512],
                    start=False, stop=True),
                    reads=[ones_bf, bout_bf], writes=[pd])
            for half in range(2):
                pd = psp[2 * tp2 + half]
                em.op("v", lambda v, tt=tt, half=half, cp=cp, pd=pd: v.tensor_tensor(
                    res_e[cp].ap()[:, tt * D + half * 512:tt * D + (half + 1) * 512],
                    res_e[cp].ap()[:, tt * D + half * 512:tt * D + (half + 1) * 512],
                    pd.ap(), OP.add),
                    reads=[pd, res_e[cp]], writes=[res_e[cp]])
            em.op("s", lambda s, ti=ti, tt=tt, cp=cp: s.dma_start(
                out=xres[ti * 128:(ti + 1) * 128, :],
                in_=res_e[cp].ap()[:, tt * D:(tt + 1) * D]),
                reads=[res_e[cp]], writes=[xres], dma=True)
            ln_stats(res_e[cp].ap()[:, tt * D:(tt + 1) * D], res_e[cp], ti, sqs.ap(), sqs)
    fence("s", [xres], xres[0:128, 0:16], F32, sbuf_src=False)
    ln_finalize()
    em.barrier()

    ffn_phase(xres, out_hbm)

    if DEBUG:
        dbg1 = nc.dram_tensor("dbg_xres", [TME, D], F32, kind="ExternalOutput").ap()
        dbg2 = nc.dram_tensor("dbg_cyg", [2, 2, 8, 128, TSEQ // 2], BF16, kind="ExternalOutput").ap()
        em.op("q", lambda q: q.dma_start(out=dbg1, in_=xres), reads=[xres], dma=True)
        for i in range(2):
            em.op("q", lambda q, i=i: q.dma_start(out=dbg2[i], in_=cyg_out[i]),
                  reads=[cyg_out[i]], dma=True)

    em.final_join()

    with nc.Block() as block:
        @block.sync
        def _(q):
            em.replay(q, "q")

        @block.vector
        def _(v):
            em.replay(v, "v")

        @block.scalar
        def _(s):
            em.replay(s, "s")

        @block.gpsimd
        def _(g):
            em.replay(g, "g")

        @block.tensor
        def _(t):
            em.replay(t, "t")

    lower_extended_insts(nc)
    return nc, stack


_PROG_CACHE = {}


def kernel(L0_attn_x, L0_attn_Win, L0_attn_bin, L0_attn_Wx, L0_attn_bx, L0_attn_Wdt,
           L0_attn_bdt, L0_attn_A_log, L0_attn_Dp, L0_attn_Wout, L0_attn_bout,
           L0_ffn_w_gate, L0_ffn_w_up, L0_ffn_w_down):
    x = np.asarray(L0_attn_x, np.float32)
    alog = np.asarray(L0_attn_A_log, np.float32)
    win = np.asarray(L0_attn_Win, np.float32)
    binv = np.asarray(L0_attn_bin, np.float32)
    wx = np.asarray(L0_attn_Wx, np.float32)
    wdt = np.asarray(L0_attn_Wdt, np.float32)
    bdt = np.asarray(L0_attn_bdt, np.float32)
    dp = np.asarray(L0_attn_Dp, np.float32)
    common = {
        "bx_t": np.ascontiguousarray(np.asarray(L0_attn_bx, np.float32).reshape(96, 1)),
        "Wout": np.ascontiguousarray(np.asarray(L0_attn_Wout, np.float32)),
        "bout_r": np.ascontiguousarray(np.asarray(L0_attn_bout, np.float32).reshape(1, 1024)),
        "w_gate": np.ascontiguousarray(np.asarray(L0_ffn_w_gate, np.float32)),
        "w_up": np.ascontiguousarray(np.asarray(L0_ffn_w_up, np.float32)),
        "w_down": np.ascontiguousarray(np.asarray(L0_ffn_w_down, np.float32)),
    }
    in_maps = []
    for c in range(8):
        b, h = c // 2, c % 2
        m = dict(common)
        m["x"] = np.ascontiguousarray(x[b, h * TME:(h + 1) * TME])
        lo, hi = h * 1024, (h + 1) * 1024
        olo, ohi = (1 - h) * 1024, (2 - h) * 1024
        m["Win"] = np.ascontiguousarray(np.concatenate(
            [win[:, lo:hi], win[:, olo:ohi], win[:, 2048 + lo:2048 + hi]], axis=1))
        binp = np.concatenate([binv[lo:hi], binv[olo:ohi], binv[2048 + lo:2048 + hi]])
        m["bin_t"] = np.ascontiguousarray(binp.reshape(24, 128).T)
        m["Wx"] = np.ascontiguousarray(np.concatenate([wx[lo:hi], wx[olo:ohi]], axis=0))
        m["Wdt"] = np.ascontiguousarray(wdt[:, lo:hi])
        m["bdt_t"] = np.ascontiguousarray(bdt[lo:hi].reshape(8, 128).T)
        m["dp_t"] = np.ascontiguousarray(dp[lo:hi].reshape(8, 128).T)
        m["alog_t"] = np.ascontiguousarray(
            alog[lo:hi].reshape(8, 128, 16).transpose(1, 0, 2).reshape(128, 128))
        m["hsel"] = np.full((1, 1), h, np.int32)
        in_maps.append(m)

    if "prog" not in _PROG_CACHE:
        _PROG_CACHE["prog"] = build_program()
    nc, _stack = _PROG_CACHE["prog"]
    res = run_bass_kernel_spmd(nc, in_maps, core_ids=list(range(8)))
    out = np.empty((4, TSEQ, D), np.float32)
    for c in range(8):
        b, h = c // 2, c % 2
        out[b, h * TME:(h + 1) * TME] = np.asarray(res.results[c]["out"], np.float32)
    if DEBUG:
        global DBG_RES
        DBG_RES = res
    return out


# version 14
# speedup vs baseline: 1.0116x; 1.0116x over previous
import contextlib
import numpy as np
import concourse.bass as bass
import concourse.mybir as mybir
from concourse.bass_utils import run_bass_kernel_spmd
from concourse.alu_op_type import AluOpType as OP
from concourse.library_overlay import lower_extended_insts

F32 = mybir.dt.float32
BF16 = mybir.dt.bfloat16
I32 = mybir.dt.int32
AF = mybir.ActivationFunctionType
AX = mybir.AxisListType

D = 1024
TME = 2048
TSEQ = 4096
FI = 2816
DI = 2048
NST = 16
NTILE = TME // 128
TCP = 512
NCH_P = TME // TCP
TCS = 512
NCH_S = TSEQ // TCS
IBLKS = [(0, 512), (512, 512), (1024, 512), (1536, 512), (2048, 512), (2560, 256)]
GROUPS = [[0, 1], [2, 3], [4, 5], [6, 7]]

_REG_DONE = [False]
DEBUG = False


def _register_custom_ops():
    if _REG_DONE[0]:
        return
    import concourse.dve_ops as dve_ops
    from concourse.dve_op import DveOp
    from concourse.dve_spec import Spec, Src0, Src1, C0, C1, C2, Zero, Bin, AluOp, sq, select, lower

    def mk(name, spec, ref):
        sha = lower(spec, ver="v3", perf_en=False, name=name).sha("v3")
        return DveOp(name, spec, subdim=False, uops_sha={"v3": sha}, reference=ref)

    x2 = sq(Src0)
    a_spec = Spec(body=Bin(AluOp.ADD,
                           Bin(AluOp.ADD, Bin(AluOp.IS_GT, x2, C0), Bin(AluOp.IS_GT, x2, C1)),
                           Bin(AluOp.IS_GT, x2, C2)))

    def a_ref(in0, in1, s0, s1, imm2):
        f = np.float64 if in0.dtype == np.float64 else np.float32
        t = (in0.astype(f)) ** 2
        return ((t > s0).astype(f) + (t > s1) + (t > imm2)).astype(f)

    s56 = Bin(AluOp.ADD, Bin(AluOp.IS_GT, x2, C1), Bin(AluOp.IS_GT, x2, C2))
    b_spec = Spec(body=Bin(AluOp.ADD, Src1,
                           Bin(AluOp.ADD, Bin(AluOp.IS_GT, x2, C0), Bin(AluOp.ADD, s56, s56))))

    def b_ref(in0, in1, s0, s1, imm2):
        f = np.float64 if in0.dtype == np.float64 else np.float32
        t = (in0.astype(f)) ** 2
        return (in1.astype(f) + (t > s0) + 2.0 * ((t > s1).astype(f) + (t > imm2))).astype(f)

    t7 = Bin(AluOp.IS_GT, x2, C0)
    d7 = Bin(AluOp.ADD, t7, t7)
    g = Bin(AluOp.ADD, Src1, Bin(AluOp.ADD, d7, d7))
    c_spec = Spec(body=select(Bin(AluOp.IS_LT, Src0, Zero), Bin(AluOp.SUBTRACT, Zero, g), g))

    def c_ref(in0, in1, s0, s1, imm2):
        f = np.float64 if in0.dtype == np.float64 else np.float32
        t = (in0.astype(f)) ** 2
        gg = in1.astype(f) + 4.0 * (t > s0).astype(f)
        return np.where(in0 < 0, -gg, gg).astype(f)

    ops = [mk("FP4A", a_spec, a_ref), mk("FP4B", b_spec, b_ref), mk("FP4C", c_spec, c_ref)]
    replace = ["DVEHASH32L_SEED", "DVEHASH32L_BODY", "DVEHASH32L_DRAIN"]
    for new_op, old_name in zip(ops, replace):
        if any(o.name == new_op.name for o in dve_ops.OPS):
            continue
        idx = next(i for i, o in enumerate(dve_ops.OPS) if o.name == old_name)
        row = dve_ops._SUB_OPCODE_FOR_NAME[old_name]
        dve_ops.OPS[idx] = new_op
        dve_ops._SUB_OPCODE_FOR_NAME[new_op.name] = row
        dve_ops.CUSTOM_DVE_SPECS[new_op.name] = new_op.spec
        dve_ops.CUSTOM_DVE_REFS[new_op.name] = new_op.ref
    _REG_DONE[0] = True


def _get_op(name):
    import concourse.dve_ops as dve_ops
    return next(o for o in dve_ops.OPS if o.name == name)


class Em:

    def __init__(self, nc, stack):
        self.nc = nc
        self.sem = {}
        for nm in ["v", "s", "g", "t", "q", "dq", "ds", "dg"]:
            self.sem[nm] = stack.enter_context(nc.semaphore("sem_" + nm))
        self.cnt = {k: 0 for k in self.sem}
        self.hw = {}
        self.writer = {}
        self.readers = {}
        self.queue = {k: [] for k in ["v", "s", "g", "t", "q"]}

    def _wait(self, e, sem, val, waits):
        if val <= 0:
            return
        key = (e, sem)
        if self.hw.get(key, 0) >= val:
            return
        waits.append((self.sem[sem], val))
        self.hw[key] = val

    def _key(self, buf):
        t = getattr(buf, "tensor", buf)
        return getattr(t, "name", str(t))

    def op(self, e, fn, reads=(), writes=(), dma=False):
        waits = []
        for b in reads:
            for sm, vv in self.writer.get(self._key(b), {}).items():
                if sm == e and e == "t":
                    continue
                self._wait(e, sm, vv, waits)
        for b in writes:
            k = self._key(b)
            for sm, vv in self.writer.get(k, {}).items():
                if sm == e and e == "t":
                    continue
                self._wait(e, sm, vv, waits)
            for (sm, vv) in self.readers.get(k, ()):
                if sm == e and e == "t":
                    continue
                self._wait(e, sm, vv, waits)
        if dma:
            sm = "d" + e
            self.cnt[sm] += 16
            done = (sm, self.cnt[sm])
            inc = (self.sem[sm], 16)
        else:
            self.cnt[e] += 1
            done = (e, self.cnt[e])
            inc = (self.sem[e], 1)
        self.queue[e].append((waits, fn, inc))
        for b in reads:
            self.readers.setdefault(self._key(b), set()).add(done)
        for b in writes:
            k = self._key(b)
            d = self.writer.setdefault(k, {})
            d[done[0]] = done[1]
            self.readers[k] = set()

    def barrier(self):
        tgt = dict(self.cnt)
        for e in ["v", "s", "g", "t", "q"]:
            waits = []
            for sm, val in tgt.items():
                if sm == e:
                    continue
                self._wait(e, sm, val, waits)
            if waits:
                self.queue[e].append((waits, None, None))

    def final_join(self):
        waits = []
        for sm, val in self.cnt.items():
            self._wait("q", sm, val, waits)
        if waits:
            self.queue["q"].append((waits, None, None))

    def replay(self, eng_handle, e):
        for waits, fn, inc in self.queue[e]:
            for semh, val in waits:
                eng_handle.wait_ge(semh, val)
            if fn is not None:
                inst = fn(eng_handle)
                inst.then_inc(inc[0], inc[1])


class Ovl:

    BASE = 28672

    def __init__(self, nc):
        self.nc = nc
        self.off = self.BASE
        self.n = 0

    def reset(self):
        self.off = self.BASE

    def t(self, name, shape, dt):
        per = int(np.prod(shape[1:])) * mybir.dt.size(dt)
        off = (self.off + 31) // 32 * 32
        self.off = off + per
        assert self.off < 225000, f"sbuf overlay overflow at {name}: {self.off}"
        self.n += 1
        return self.nc.alloc_sbuf_tensor_at(f"{name}_{self.n}", shape, dt, offset=off)


def _cc(g, kind, ins, outs):
    return g.collective_compute(kind, mybir.AluOpType.bypass,
                                replica_groups=GROUPS, ins=ins, outs=outs)


def build_program():
    _register_custom_ops()
    op_a, op_b, op_c = _get_op("FP4A"), _get_op("FP4B"), _get_op("FP4C")
    stack = contextlib.ExitStack()
    nc = bass.Bass("TRN2", target_bir_lowering=False)

    din = {}
    for nm, shape in [("x", [TME, D]), ("Win", [D, 3072]), ("bin_t", [128, 24]),
                      ("Wx", [DI, 96]), ("bx_t", [96, 1]), ("Wdt", [64, 1024]),
                      ("bdt_t", [128, 8]), ("alog_t", [128, 128]), ("dp_t", [128, 8]),
                      ("Wout", [DI, D]), ("bout_r", [1, D]),
                      ("w_gate", [FI, D]), ("w_up", [FI, D]), ("w_down", [D, FI])]:
        din[nm] = nc.dram_tensor(nm, shape, F32, kind="ExternalInput").ap()
    din["hsel"] = nc.dram_tensor("hsel", [1, 1], I32, kind="ExternalInput").ap()
    out_hbm = nc.dram_tensor("out", [TME, D], F32, kind="ExternalOutput").ap()

    wqgT = nc.dram_tensor("wqgT", [8, 128, 22, 128], BF16, kind="Internal").ap()
    wquT = nc.dram_tensor("wquT", [8, 128, 22, 128], BF16, kind="Internal").ap()
    wqdT = nc.dram_tensor("wqdT", [22, 128, 8, 128], BF16, kind="Internal").ap()
    win_bf = nc.dram_tensor("win_bf", [D, 3072], BF16, kind="Internal").ap()
    wout_bf = nc.dram_tensor("wout_bf", [DI, D], BF16, kind="Internal").ap()
    xres = nc.dram_tensor("xres", [TME, D], F32, kind="Internal").ap()
    xqt_sp = nc.dram_tensor("xqt_sp", [TME, D], BF16, kind="Internal").ap()
    hqt_sp = nc.dram_tensor("hqt_sp", [TME, FI], BF16, kind="Internal").ap()
    cxl_in = [nc.dram_tensor(f"cxl_in{i}", [8, 128, TME // 2], BF16, kind="Internal").ap()
              for i in range(2)]
    cxl_out = [nc.dram_tensor(f"cxl_out{i}", [2, 8, 128, TME // 2], BF16, kind="Internal").ap()
               for i in range(2)]
    cyg_in = [nc.dram_tensor(f"cyg_in{i}", [8, 128, TSEQ // 2], BF16, kind="Internal").ap()
              for i in range(2)]
    cyg_out = [nc.dram_tensor(f"cyg_out{i}", [2, 8, 128, TSEQ // 2], BF16, kind="Internal").ap()
               for i in range(2)]
    bc_dram = nc.dram_tensor("bc_dram", [32, TCS], BF16, kind="Internal").ap()

    em = Em(nc, stack)
    ovl = Ovl(nc)

    A = lambda name, shape, dt: nc.alloc_sbuf_tensor(name, shape, dt)
    fsk_dr = {F32: nc.dram_tensor("fsk_dr_f", [128, 16], F32, kind="Internal").ap(),
              BF16: nc.dram_tensor("fsk_dr_h", [128, 16], BF16, kind="Internal").ap()}
    fsk_sb = {F32: A("fsk_sb_f", [128, 16], F32),
              BF16: A("fsk_sb_h", [128, 16], BF16)}

    def fence(e, bufs, src_ap, dt, sbuf_src=True):
        sink = fsk_dr[dt] if sbuf_src else fsk_sb[dt].ap()[:, 0:16]
        em.op(e, lambda eng, sink=sink, src_ap=src_ap: eng.dma_start(
            out=sink, in_=src_ap), reads=[], writes=list(bufs), dma=True)
    ident = A("ident", [128, 128], BF16)
    ii32 = A("ii32", [128, 128], I32)
    ones_col = A("ones_col", [128, 1], F32)
    ones_bf = A("ones_bf", [1, 128], BF16)
    rstd_a = A("rstd_a", [128, NTILE], F32)
    nmr_a = A("nmr_a", [128, NTILE], F32)
    sums_a = A("sums_a", [128, NTILE], F32)
    ssq_a = A("ssq_a", [128, NTILE], F32)
    stat_t = A("stat_t", [128, NTILE], F32)
    stat_u = A("stat_u", [128, NTILE], F32)
    bin_sb = A("bin_sb", [128, 24], F32)
    bx_sb = A("bx_sb", [96, 1], F32)
    bdt_sb = A("bdt_sb", [128, 8], F32)
    a_sb = A("a_sb", [128, 128], F32)
    dp_sb = A("dp_sb", [128, 8], F32)
    hsel_sb = A("hsel_sb", [1, 1], I32)
    bout_sb = A("bout_sb", [1, D], F32)
    bout_bf = A("bout_bf", [1, D], BF16)
    state = A("state", [128, 128], F32)

    psp = [stack.enter_context(nc.psum_tensor(f"ps_{i}", [128, 512], F32))
           for i in range(4)]
    pst = [stack.enter_context(nc.psum_tensor(f"pst_{i}", [128, 1024], BF16))
           for i in range(2)]

    def quant_chain(P, Fdim, src_f32, dst_bf16, tmp, pre=None):
        nb = Fdim // 16
        m, r6, xn, p1, p2 = tmp["m"], tmp["r6"], tmp["xn"], tmp["p1"], tmp["p2"]
        q = p1
        src3 = src_f32.rearrange("p (a b) -> p a b", b=16)
        if pre == "g":
            mn = tmp["mn"]
            em.op("g", lambda g: g.tensor_reduce(m[:, 0:nb], src3, AX.X, OP.max),
                  reads=[src_f32], writes=[m])
            em.op("g", lambda g: g.tensor_reduce(mn[:, 0:nb], src3, AX.X, OP.min),
                  reads=[src_f32], writes=[mn])
            em.op("v", lambda v: v.scalar_tensor_tensor(
                m[:, 0:nb], mn[:, 0:nb], -1.0, m[:, 0:nb], OP.mult, OP.max),
                reads=[mn, m], writes=[m])
        else:
            em.op("v", lambda v: v.tensor_reduce(m[:, 0:nb], src3, AX.X, OP.max,
                                                 apply_absolute_value=True),
                  reads=[src_f32], writes=[m])
        em.op("v", lambda v: v.tensor_scalar(m[:, 0:nb], m[:, 0:nb], 6e-12, None, OP.max),
              reads=[m], writes=[m])
        em.op("v", lambda v: v.reciprocal_approx_fast(out=r6[:, 0:nb], in_=m[:, 0:nb]),
              reads=[m], writes=[r6])
        em.op("v", lambda v: v.tensor_scalar(r6[:, 0:nb], r6[:, 0:nb], 6.0, None, OP.mult),
              reads=[r6], writes=[r6])
        r6b = r6[:, 0:nb].unsqueeze(2).broadcast_to([P, nb, 16])
        em.op("v", lambda v: v.tensor_tensor(xn[:, 0:Fdim].rearrange("p (a b) -> p a b", b=16),
                                             src3, r6b, OP.mult),
              reads=[src_f32, r6], writes=[xn])
        em.op("v", lambda v: v._custom_dve(op_a, out=p1[:, 0:Fdim], in0=xn[:, 0:Fdim],
                                           s0=0.0625, s1=0.5625, imm2=1.5625),
              reads=[xn], writes=[p1])
        em.op("v", lambda v: v._custom_dve(op_b, out=p2[:, 0:Fdim], in0=xn[:, 0:Fdim],
                                           in1=p1[:, 0:Fdim], s0=3.0625, s1=6.25, imm2=12.25),
              reads=[xn, p1], writes=[p2])
        em.op("v", lambda v: v._custom_dve(op_c, out=q[:, 0:Fdim], in0=xn[:, 0:Fdim],
                                           in1=p2[:, 0:Fdim], s0=25.0, s1=0.0, imm2=0.0),
              reads=[xn, p2], writes=[p1])
        em.op("v", lambda v: v.tensor_scalar(r6[:, 0:nb], m[:, 0:nb], 1.0 / 12.0, None, OP.mult),
              reads=[m], writes=[r6])
        em.op("v", lambda v: v.tensor_tensor(dst_bf16.rearrange("p (a b) -> p a b", b=16),
                                             q[:, 0:Fdim].rearrange("p (a b) -> p a b", b=16),
                                             r6[:, 0:nb].unsqueeze(2).broadcast_to([P, nb, 16]),
                                             OP.mult),
              reads=[p1, r6], writes=[dst_bf16])

    def ln_stats(src_ap, src_buf, ti, xtmp, xtmp_buf):
        em.op("v", lambda v: v.tensor_reduce(sums_a.ap()[:, ti:ti + 1], src_ap, AX.X, OP.add),
              reads=[src_buf], writes=[sums_a])
        em.op("s", lambda s: s.activation(xtmp, src_ap, AF.Square,
                                          accum_out=ssq_a.ap()[:, ti:ti + 1]),
              reads=[src_buf], writes=[ssq_a, xtmp_buf])

    def ln_finalize():
        em.op("v", lambda v: v.tensor_scalar(stat_t.ap(), sums_a.ap(), 1.0 / D, None, OP.mult),
              reads=[sums_a], writes=[stat_t])
        em.op("v", lambda v: v.tensor_tensor(stat_u.ap(), stat_t.ap(), stat_t.ap(), OP.mult),
              reads=[stat_t], writes=[stat_u])
        em.op("v", lambda v: v.scalar_tensor_tensor(stat_u.ap(), ssq_a.ap(), 1.0 / D,
                                                    stat_u.ap(), OP.mult, OP.subtract),
              reads=[ssq_a, stat_u], writes=[stat_u])
        em.op("v", lambda v: v.tensor_scalar(stat_u.ap(), stat_u.ap(), 1e-5, None, OP.add),
              reads=[stat_u], writes=[stat_u])
        em.op("v", lambda v: v.reciprocal(rstd_a.ap(), stat_u.ap()),
              reads=[stat_u], writes=[rstd_a])
        em.op("s", lambda s: s.activation(rstd_a.ap(), rstd_a.ap(), AF.Sqrt),
              reads=[rstd_a], writes=[rstd_a])
        em.op("v", lambda v: v.tensor_tensor(nmr_a.ap(), rstd_a.ap(), rstd_a.ap(), OP.mult),
              reads=[rstd_a], writes=[nmr_a])
        em.op("v", lambda v: v.tensor_tensor(nmr_a.ap(), nmr_a.ap(), stat_u.ap(), OP.mult),
              reads=[nmr_a, stat_u], writes=[nmr_a])
        em.op("v", lambda v: v.tensor_scalar(nmr_a.ap(), nmr_a.ap(), -0.5, 1.5, OP.mult, OP.add),
              reads=[nmr_a], writes=[nmr_a])
        em.op("v", lambda v: v.tensor_tensor(rstd_a.ap(), rstd_a.ap(), nmr_a.ap(), OP.mult),
              reads=[rstd_a, nmr_a], writes=[rstd_a])
        em.op("v", lambda v: v.scalar_tensor_tensor(nmr_a.ap(), stat_t.ap(), -1.0,
                                                    rstd_a.ap(), OP.mult, OP.mult),
              reads=[stat_t, rstd_a], writes=[nmr_a])

    em.op("g", lambda g: g.iota(ii32.ap(), pattern=[[1, 128]], base=0, channel_multiplier=-1),
          writes=[ii32])
    em.op("v", lambda v: v.tensor_scalar(ident.ap(), ii32.ap(), 0, None, OP.is_equal),
          reads=[ii32], writes=[ident])
    em.op("v", lambda v: v.memset(ones_col.ap(), 1.0), writes=[ones_col])
    em.op("v", lambda v: v.memset(ones_bf.ap(), 1.0), writes=[ones_bf])
    em.op("v", lambda v: v.memset(state.ap(), 0.0), writes=[state])
    for nm, dst in [("bin_t", bin_sb), ("bx_t", bx_sb), ("bdt_t", bdt_sb),
                    ("alog_t", a_sb), ("dp_t", dp_sb), ("bout_r", bout_sb),
                    ("hsel", hsel_sb)]:
        em.op("q", lambda q, nm=nm, dst=dst: q.dma_start(out=dst.ap(), in_=din[nm]),
              writes=[dst], dma=True)
    em.op("s", lambda s: s.activation(a_sb.ap(), a_sb.ap(), AF.Exp),
          reads=[a_sb], writes=[a_sb])
    em.op("v", lambda v: v.tensor_scalar(a_sb.ap(), a_sb.ap(), -1.0, None, OP.mult),
          reads=[a_sb], writes=[a_sb])
    em.op("v", lambda v: v.tensor_copy(bout_bf.ap(), bout_sb.ap()),
          reads=[bout_sb], writes=[bout_bf])

    ovl.reset()
    wload = [ovl.t("wload", [128, 2 * DI], F32) for _ in range(2)]
    wq16 = [ovl.t("wq16", [128, FI], BF16) for _ in range(2)]
    wcast = ovl.t("wcast", [128, 2 * DI], BF16)
    wstage = [ovl.t("wstage", [128, 1024], BF16) for _ in range(2)]
    qm = ovl.t("qm", [128, 256], F32)
    qmn = ovl.t("qmn", [128, 256], F32)
    qr6 = ovl.t("qr6", [128, 256], F32)
    qxn = ovl.t("qxn", [128, FI], F32)
    qp1 = ovl.t("qp1", [128, FI], F32)
    qp2 = ovl.t("qp2", [128, FI], F32)
    tmp = {"m": qm.ap(), "mn": qmn.ap(), "r6": qr6.ap(), "xn": qxn.ap(),
           "p1": qp1.ap(), "p2": qp2.ap()}

    def transpose_blocks_to_hbm(src_sb, n_blk, dst_hbm, dst_rfn, rr=[0]):
        for j0 in range(0, n_blk, 8):
            jn = min(8, n_blk - j0)
            pt = pst[rr[0] % 2]
            ws = wstage[rr[0] % 2]
            rr[0] += 1
            for jj in range(jn):
                j = j0 + jj
                em.op("t", lambda t, j=j, jj=jj, pt=pt: t.transpose(
                    pt.ap()[:, jj * 128:(jj + 1) * 128],
                    src_sb.ap()[:, j * 128:(j + 1) * 128], ident.ap()),
                    reads=[src_sb, ident], writes=[pt])
            em.op("v", lambda v, jn=jn, pt=pt, ws=ws: v.tensor_copy(
                ws.ap()[:, 0:jn * 128], pt.ap()[:, 0:jn * 128]),
                reads=[pt], writes=[ws])
            em.op("s", lambda s, j0=j0, jn=jn, ws=ws: s.dma_start(
                out=dst_rfn(j0, jn),
                in_=ws.ap()[:, 0:jn * 128].rearrange("p (j t) -> p j t", t=128)),
                reads=[ws], writes=[dst_hbm], dma=True)

    wi = [0]
    for wname, wdst in [("w_gate", wqgT), ("w_up", wquT)]:
        for k in range(22):
            p = wi[0] % 2
            wi[0] += 1
            em.op("q", lambda q, wname=wname, k=k, p=p: q.dma_start(
                out=wload[p].ap()[:, 0:D], in_=din[wname][k * 128:(k + 1) * 128, :]),
                writes=[wload[p]], dma=True)
            fence("q", [wload[p]], wload[p].ap()[:, 0:16], F32)
            quant_chain(128, D, wload[p].ap()[:, 0:D], wq16[p].ap()[:, 0:D], tmp)
            transpose_blocks_to_hbm(wq16[p], 8, wdst,
                                    lambda j0, jn, k=k, wdst=wdst:
                                    wdst[j0:j0 + jn, :, k, :].transpose([1, 0, 2]))
    for k in range(8):
        p = wi[0] % 2
        wi[0] += 1
        em.op("q", lambda q, k=k, p=p: q.dma_start(
            out=wload[p].ap()[:, 0:FI], in_=din["w_down"][k * 128:(k + 1) * 128, :]),
            writes=[wload[p]], dma=True)
        fence("q", [wload[p]], wload[p].ap()[:, 0:16], F32)
        quant_chain(128, FI, wload[p].ap()[:, 0:FI], wq16[p].ap()[:, 0:FI], tmp)
        transpose_blocks_to_hbm(wq16[p], 22, wqdT,
                                lambda j0, jn, k=k: wqdT[j0:j0 + jn, :, k, :].transpose([1, 0, 2]))
    for k in range(8):
        em.op("q", lambda q, k=k: q.dma_start(
            out=wload[0].ap()[:, 0:3072], in_=din["Win"][k * 128:(k + 1) * 128, :]),
            writes=[wload[0]], dma=True)
        fence("q", [wload[0]], wload[0].ap()[:, 0:16], F32)
        em.op("v", lambda v: v.tensor_copy(wcast.ap()[:, 0:3072], wload[0].ap()[:, 0:3072]),
              reads=[wload[0]], writes=[wcast])
        em.op("s", lambda s, k=k: s.dma_start(
            out=win_bf[k * 128:(k + 1) * 128, :], in_=wcast.ap()[:, 0:3072]),
            reads=[wcast], writes=[win_bf], dma=True)
    for k in range(16):
        em.op("q", lambda q, k=k: q.dma_start(
            out=wload[0].ap()[:, 0:D], in_=din["Wout"][k * 128:(k + 1) * 128, :]),
            writes=[wload[0]], dma=True)
        fence("q", [wload[0]], wload[0].ap()[:, 0:16], F32)
        em.op("v", lambda v: v.tensor_copy(wcast.ap()[:, 0:D], wload[0].ap()[:, 0:D]),
              reads=[wload[0]], writes=[wcast])
        em.op("s", lambda s, k=k: s.dma_start(
            out=wout_bf[k * 128:(k + 1) * 128, :], in_=wcast.ap()[:, 0:D]),
            reads=[wcast], writes=[wout_bf], dma=True)

    _ffn_n = [0]

    def ffn_phase(src_hbm, dst_hbm):
        _ffn_n[0] += 1
        hi = 190016
        fx0 = [nc.alloc_sbuf_tensor_at(f"fx0_h{_ffn_n[0]}_{i}", [128, D], F32,
                                       offset=hi + i * 4096) for i in range(2)]
        fsq = nc.alloc_sbuf_tensor_at(f"fsq_h{_ffn_n[0]}", [128, D], F32,
                                      offset=hi + 2 * 4096)
        for ti in range(NTILE):
            p = ti % 2
            em.op("q", lambda q, ti=ti, p=p: q.dma_start(
                out=fx0[p].ap(), in_=src_hbm[ti * 128:(ti + 1) * 128, :]),
                reads=[src_hbm], writes=[fx0[p]], dma=True)
            fence("q", [fx0[p]], fx0[p].ap()[:, 0:16], F32)
            ln_stats(fx0[p].ap(), fx0[p], ti, fsq.ap(), fsq)
        ln_finalize()
        em.barrier()
        ovl.reset()
        fx = [ovl.t("fx", [128, D], F32) for _ in range(2)]
        fln = [ovl.t("fln", [128, D], F32) for _ in range(2)]
        fq = [ovl.t("fq", [128, D], BF16) for _ in range(2)]
        fqt = [ovl.t("fqt", [128, 8 * 128], BF16) for _ in range(2)]
        qm = ovl.t("qm", [128, 64], F32)
        qmn = ovl.t("qmn", [128, 64], F32)
        qr6 = ovl.t("qr6", [128, 64], F32)
        qxn = ovl.t("qxn", [128, D], F32)
        qp1 = ovl.t("qp1", [128, D], F32)
        qp2 = ovl.t("qp2", [128, D], F32)
        tmp = {"m": qm.ap(), "mn": qmn.ap(), "r6": qr6.ap(), "xn": qxn.ap(),
               "p1": qp1.ap(), "p2": qp2.ap()}
        for ti in range(NTILE):
            p = ti % 2
            em.op("q", lambda q, ti=ti, p=p: q.dma_start(
                out=fx[p].ap(), in_=src_hbm[ti * 128:(ti + 1) * 128, :]),
                reads=[src_hbm], writes=[fx[p]], dma=True)
            fence("q", [fx[p]], fx[p].ap()[:, 0:16], F32)
            em.op("v", lambda v, ti=ti, p=p: v.tensor_scalar(
                fln[p].ap(), fx[p].ap(), rstd_a.ap()[:, ti:ti + 1],
                nmr_a.ap()[:, ti:ti + 1], OP.mult, OP.add),
                reads=[fx[p], rstd_a, nmr_a], writes=[fln[p]])
            quant_chain(128, D, fln[p].ap(), fq[p].ap(), tmp)
            for k in range(8):
                em.op("t", lambda t, k=k, p=p: t.transpose(
                    pst[p].ap()[:, k * 128:(k + 1) * 128],
                    fq[p].ap()[:, k * 128:(k + 1) * 128], ident.ap()),
                    reads=[fq[p], ident], writes=[pst[p]])
            em.op("v", lambda v, p=p: v.tensor_copy(fqt[p].ap(), pst[p].ap()),
                  reads=[pst[p]], writes=[fqt[p]])
            em.op("s", lambda s, ti=ti, p=p: s.dma_start(
                out=xqt_sp[ti * 128:(ti + 1) * 128, :], in_=fqt[p].ap()),
                reads=[fqt[p]], writes=[xqt_sp], dma=True)
        em.barrier()
        ovl.reset()
        wg_sb = ovl.t("wg_sb", [128, 8 * FI], BF16)
        wu_sb = ovl.t("wu_sb", [128, 8 * FI], BF16)
        xqt_t = [ovl.t("xqt_t", [128, 8 * 128], BF16) for _ in range(2)]
        sg = [ovl.t("sg", [128, 512], F32) for _ in range(2)]
        h_sb = [ovl.t("h_sb", [128, FI], F32) for _ in range(2)]
        hq = [ovl.t("hq", [128, FI], BF16) for _ in range(2)]
        hqt = [ovl.t("hqt", [128, 22 * 128], BF16) for _ in range(2)]
        qm = ovl.t("qm", [128, 176], F32)
        qmn = ovl.t("qmn", [128, 176], F32)
        qr6 = ovl.t("qr6", [128, 176], F32)
        qxn = ovl.t("qxn", [128, FI], F32)
        qp1 = ovl.t("qp1", [128, FI], F32)
        qp2 = ovl.t("qp2", [128, FI], F32)
        tmp = {"m": qm.ap(), "mn": qmn.ap(), "r6": qr6.ap(), "xn": qxn.ap(),
               "p1": qp1.ap(), "p2": qp2.ap()}
        for k in range(8):
            em.op("q", lambda q, k=k: q.dma_start(
                out=wg_sb.ap()[:, k * FI:(k + 1) * FI],
                in_=wqgT[k].rearrange("p a t -> p (a t)")),
                reads=[wqgT], writes=[wg_sb], dma=True)
            em.op("q", lambda q, k=k: q.dma_start(
                out=wu_sb.ap()[:, k * FI:(k + 1) * FI],
                in_=wquT[k].rearrange("p a t -> p (a t)")),
                reads=[wquT], writes=[wu_sb], dma=True)
        fence("q", [wg_sb, wu_sb], wu_sb.ap()[:, 0:16], BF16)
        pending_tr = [None]

        def flush_tr():
            if pending_tr[0] is not None:
                pending_tr[0]()
                pending_tr[0] = None

        for ti in range(NTILE):
            p = ti % 2
            em.op("q", lambda q, ti=ti, p=p: q.dma_start(
                out=xqt_t[p].ap(), in_=xqt_sp[ti * 128:(ti + 1) * 128, :]),
                reads=[xqt_sp], writes=[xqt_t[p]], dma=True)
            fence("q", [xqt_t[p]], xqt_t[p].ap()[:, 0:16], BF16)
            for bi, (o0, wd) in enumerate(IBLKS):
                pg = psp[2 * (bi % 2)]
                pu = psp[2 * (bi % 2) + 1]
                for k in range(8):
                    em.op("t", lambda t, k=k, o0=o0, wd=wd, p=p, pg=pg: t.matmul(
                        pg.ap()[:, 0:wd],
                        xqt_t[p].ap()[:, k * 128:(k + 1) * 128],
                        wg_sb.ap()[:, k * FI + o0:k * FI + o0 + wd],
                        start=(k == 0), stop=(k == 7)),
                        reads=[xqt_t[p], wg_sb], writes=[pg])
                for k in range(8):
                    em.op("t", lambda t, k=k, o0=o0, wd=wd, p=p, pu=pu: t.matmul(
                        pu.ap()[:, 0:wd],
                        xqt_t[p].ap()[:, k * 128:(k + 1) * 128],
                        wu_sb.ap()[:, k * FI + o0:k * FI + o0 + wd],
                        start=(k == 0), stop=(k == 7)),
                        reads=[xqt_t[p], wu_sb], writes=[pu])
                em.op("s", lambda s, wd=wd, p=p, pg=pg: s.activation(
                    sg[p].ap()[:, 0:wd], pg.ap()[:, 0:wd], AF.Silu),
                    reads=[pg], writes=[sg[p]])
                em.op("v", lambda v, o0=o0, wd=wd, p=p, pu=pu: v.tensor_tensor(
                    h_sb[p].ap()[:, o0:o0 + wd], sg[p].ap()[:, 0:wd],
                    pu.ap()[:, 0:wd], OP.mult),
                    reads=[sg[p], pu], writes=[h_sb[p]])
            quant_chain(128, FI, h_sb[p].ap(), hq[p].ap(), tmp)
            flush_tr()

            def do_tr(ti=ti, p=p):
                for j0 in range(0, 22, 8):
                    jn = min(8, 22 - j0)
                    pt = pst[(j0 // 8) % 2]
                    for jj in range(jn):
                        j = j0 + jj
                        em.op("t", lambda t, j=j, jj=jj, p=p, pt=pt: t.transpose(
                            pt.ap()[:, jj * 128:(jj + 1) * 128],
                            hq[p].ap()[:, j * 128:(j + 1) * 128], ident.ap()),
                            reads=[hq[p], ident], writes=[pt])
                    em.op("v", lambda v, j0=j0, jn=jn, p=p, pt=pt: v.tensor_copy(
                        hqt[p].ap()[:, j0 * 128:(j0 + jn) * 128], pt.ap()[:, 0:jn * 128]),
                        reads=[pt], writes=[hqt[p]])
                em.op("s", lambda s, ti=ti, p=p: s.dma_start(
                    out=hqt_sp[ti * 128:(ti + 1) * 128, :], in_=hqt[p].ap()),
                    reads=[hqt[p]], writes=[hqt_sp], dma=True)

            pending_tr[0] = do_tr
        flush_tr()
        em.barrier()
        ovl.reset()
        wd_sb = ovl.t("wd_sb", [128, 22 * D], BF16)
        hqt2 = [ovl.t("hqt2", [128, 22 * 128], BF16) for _ in range(2)]
        fx2 = [ovl.t("fx2", [128, D], F32) for _ in range(2)]
        ot = [ovl.t("ot", [128, D], F32) for _ in range(2)]
        for k in range(22):
            em.op("q", lambda q, k=k: q.dma_start(
                out=wd_sb.ap()[:, k * D:(k + 1) * D],
                in_=wqdT[k].rearrange("p a t -> p (a t)")),
                reads=[wqdT], writes=[wd_sb], dma=True)
        fence("q", [wd_sb], wd_sb.ap()[:, 0:16], BF16)
        for ti in range(NTILE):
            p = ti % 2
            em.op("q", lambda q, ti=ti, p=p: q.dma_start(
                out=hqt2[p].ap(), in_=hqt_sp[ti * 128:(ti + 1) * 128, :]),
                reads=[hqt_sp], writes=[hqt2[p]], dma=True)
            em.op("q", lambda q, ti=ti, p=p: q.dma_start(
                out=fx2[p].ap(), in_=src_hbm[ti * 128:(ti + 1) * 128, :]),
                reads=[src_hbm], writes=[fx2[p]], dma=True)
            fence("q", [hqt2[p], fx2[p]], fx2[p].ap()[:, 0:16], F32)
            for half in range(2):
                pd = psp[2 * p + half]
                for k in range(22):
                    em.op("t", lambda t, k=k, half=half, p=p, pd=pd: t.matmul(
                        pd.ap(),
                        hqt2[p].ap()[:, k * 128:(k + 1) * 128],
                        wd_sb.ap()[:, k * D + half * 512:k * D + (half + 1) * 512],
                        start=(k == 0), stop=(k == 21)),
                        reads=[hqt2[p], wd_sb], writes=[pd])
            for half in range(2):
                pd = psp[2 * p + half]
                em.op("v", lambda v, half=half, p=p, pd=pd: v.scalar_tensor_tensor(
                    ot[p].ap()[:, half * 512:(half + 1) * 512], pd.ap(), 0.5,
                    fx2[p].ap()[:, half * 512:(half + 1) * 512], OP.mult, OP.add),
                    reads=[pd, fx2[p]], writes=[ot[p]])
            em.op("s", lambda s, ti=ti, p=p: s.dma_start(
                out=dst_hbm[ti * 128:(ti + 1) * 128, :], in_=ot[p].ap()),
                reads=[ot[p]], writes=[dst_hbm], dma=True)
        fence("s", [dst_hbm], dst_hbm[0:128, 0:16], F32, sbuf_src=False)
        em.barrier()

    ffn_phase(din["x"], xres)

    ovl.reset()
    fx0 = [ovl.t("fx0", [128, D], F32) for _ in range(2)]
    fsq0 = ovl.t("fsq0", [128, D], F32)
    for ti in range(NTILE):
        p = ti % 2
        em.op("q", lambda q, ti=ti, p=p: q.dma_start(
            out=fx0[p].ap(), in_=xres[ti * 128:(ti + 1) * 128, :]),
            reads=[xres], writes=[fx0[p]], dma=True)
        fence("q", [fx0[p]], fx0[p].ap()[:, 0:16], F32)
        ln_stats(fx0[p].ap(), fx0[p], ti, fsq0.ap(), fsq0)
    ln_finalize()
    em.barrier()

    ovl.reset()
    res_c = [ovl.t("res_c", [128, D], F32) for _ in range(2)]
    xln_f = [ovl.t("xln_f", [128, D], BF16) for _ in range(2)]
    xlt_st = [ovl.t("xlt_st", [128, 8 * 128], BF16) for _ in range(2)]
    for ti in range(NTILE):
        p = ti % 2
        em.op("q", lambda q, ti=ti, p=p: q.dma_start(
            out=res_c[p].ap(), in_=xres[ti * 128:(ti + 1) * 128, :]),
            reads=[xres], writes=[res_c[p]], dma=True)
        fence("q", [res_c[p]], res_c[p].ap()[:, 0:16], F32)
        em.op("v", lambda v, ti=ti, p=p: v.tensor_scalar(
            xln_f[p].ap(), res_c[p].ap(),
            rstd_a.ap()[:, ti:ti + 1], nmr_a.ap()[:, ti:ti + 1], OP.mult, OP.add),
            reads=[res_c[p], rstd_a, nmr_a], writes=[xln_f[p]])
        for k in range(8):
            em.op("t", lambda t, k=k, p=p: t.transpose(
                pst[p].ap()[:, k * 128:(k + 1) * 128],
                xln_f[p].ap()[:, k * 128:(k + 1) * 128], ident.ap()),
                reads=[xln_f[p], ident], writes=[pst[p]])
        em.op("v", lambda v, p=p: v.tensor_copy(xlt_st[p].ap(), pst[p].ap()),
              reads=[pst[p]], writes=[xlt_st[p]])
        half = 0 if ti < 8 else 1
        tci = (ti % 8) * 128
        em.op("s", lambda s, ti=ti, p=p, half=half, tci=tci: s.dma_start(
            out=cxl_in[half].transpose([1, 0, 2])[:, :, tci:tci + 128],
            in_=xlt_st[p].ap().rearrange("p (k t) -> p k t", t=128)),
            reads=[xlt_st[p]], writes=[cxl_in[half]], dma=True)
        if ti == 7:
            fence("s", [cxl_in[0]], cxl_in[0][0, 0:128, 0:16], BF16, sbuf_src=False)
            em.op("g", lambda g: _cc(g, "AllGather", [cxl_in[0]], [cxl_out[0]]),
                  reads=[cxl_in[0]], writes=[cxl_out[0]])
    fence("s", [cxl_in[1]], cxl_in[1][0, 0:128, 0:16], BF16, sbuf_src=False)
    em.op("g", lambda g: _cc(g, "AllGather", [cxl_in[1]], [cxl_out[1]]),
          reads=[cxl_in[1]], writes=[cxl_out[1]])
    em.barrier()

    ovl.reset()
    win_sb = ovl.t("win_sb", [128, 8 * 3072], BF16)
    wx_sb = ovl.t("wx_sb", [128, 16 * 96], BF16)
    wdt_sb = ovl.t("wdt_sb", [64, 1024], BF16)
    wldf = ovl.t("wldf", [128, 16 * 96], F32)
    xlnt = [ovl.t("xlnt", [128, 8 * TCS], BF16)] * 2
    xs_c = ovl.t("xs_c", [128, 16 * TCS], BF16)
    z_c = ovl.t("z_c", [128, 8 * TCS], BF16)
    xp_sb = ovl.t("xp_sb", [128, TCS], F32)
    bc16 = ovl.t("bc16", [128, TCS], BF16)
    bcb = ovl.t("bcb", [128, 16 * TCS], BF16)
    bcc = ovl.t("bcc", [128, 16 * TCS], BF16)
    delta_t = [ovl.t("delta_t", [128, TCS], F32) for _ in range(2)]
    sp_t1 = [ovl.t("sp_t1", [128, TCS], F32) for _ in range(2)]
    sp_t2 = [ovl.t("sp_t2", [128, TCS], F32) for _ in range(2)]
    u_t = [ovl.t("u_t", [128, TCS], BF16) for _ in range(2)]
    da2 = [ovl.t(f"da2_{i}", [128, TCS], F32) for i in range(2)]
    w2 = [ovl.t(f"w2_{i}", [128, TCS], BF16) for i in range(2)]
    h2 = [ovl.t(f"h2_{i}", [128, TCS], F32) for i in range(2)]
    hp_all = [ovl.t("hp_all", [128, 16 * TCS], BF16) for _ in range(2)]
    y_s = ovl.t("y_s", [128, TCS], F32)
    yg_t = ovl.t("yg_t", [128, TCS], F32)
    yg2_t = [ovl.t("yg2_t", [128, TCS], BF16) for _ in range(2)]
    em.op("q", lambda q: q.dma_start(
        out=wldf.ap()[:, 0:16 * 96].rearrange("p (k r) -> p k r", r=96),
        in_=din["Wx"].rearrange("(k p) r -> p k r", p=128)),
        writes=[wldf], dma=True)
    fence("q", [wldf], wldf.ap()[:, 0:16], F32)
    em.op("v", lambda v: v.tensor_copy(wx_sb.ap(), wldf.ap()[:, 0:16 * 96]),
          reads=[wldf], writes=[wx_sb])
    em.op("q", lambda q: q.dma_start(out=wldf.ap()[0:64, 0:1024], in_=din["Wdt"]),
          writes=[wldf], dma=True)
    fence("q", [wldf], wldf.ap()[:, 0:16], F32)
    em.op("v", lambda v: v.tensor_copy(wdt_sb.ap(), wldf.ap()[0:64, 0:1024]),
          reads=[wldf], writes=[wdt_sb])
    for k in range(8):
        em.op("q", lambda q, k=k: q.dma_start(
            out=win_sb.ap()[:, k * 3072:(k + 1) * 3072],
            in_=win_bf[k * 128:(k + 1) * 128, :]),
            reads=[win_bf], writes=[win_sb], dma=True)
    fence("q", [win_sb], win_sb.ap()[:, 0:16], BF16)

    for ci in range(NCH_S):
        slot = ci // (NCH_S // 2)
        toff = (ci % (NCH_S // 2)) * TCS
        gt0 = ci * TCS
        cp = ci % 2
        xh = 0 if toff < TME // 2 else 1
        toff2 = toff % (TME // 2)
        em.op("q", lambda q, slot=slot, toff2=toff2, cp=cp, xh=xh: q.dma_start(
            out=xlnt[cp].ap().rearrange("p (k t) -> p k t", t=TCS),
            in_=cxl_out[xh][slot, :, :, toff2:toff2 + TCS].transpose([1, 0, 2])),
            reads=[cxl_out[xh]], writes=[xlnt[cp]], dma=True)
        fence("q", [xlnt[cp]], xlnt[cp].ap()[:, 0:16], BF16)
        for fc in range(24):
            pw = psp[fc % 4]
            for k in range(8):
                em.op("t", lambda t, k=k, fc=fc, cp=cp, pw=pw: t.matmul(
                    pw.ap()[:, 0:TCS],
                    win_sb.ap().rearrange("p (k f) -> p k f", f=3072)[:, k, fc * 128:(fc + 1) * 128],
                    xlnt[cp].ap()[:, k * TCS:(k + 1) * TCS],
                    start=(k == 0), stop=(k == 7)),
                    reads=[win_sb, xlnt[cp]], writes=[pw])
            if fc < 16:
                em.op("s", lambda s, fc=fc, pw=pw: s.activation(
                    xs_c.ap()[:, fc * TCS:(fc + 1) * TCS], pw.ap()[:, 0:TCS], AF.Silu,
                    bias=bin_sb.ap()[:, fc:fc + 1]),
                    reads=[pw, bin_sb], writes=[xs_c])
            else:
                em.op("s", lambda s, fc=fc, pw=pw: s.activation(
                    z_c.ap()[:, (fc - 16) * TCS:(fc - 15) * TCS], pw.ap()[:, 0:TCS], AF.Silu,
                    bias=bin_sb.ap()[:, fc:fc + 1]),
                    reads=[pw, bin_sb], writes=[z_c])
        for k in range(16):
            em.op("t", lambda t, k=k: t.matmul(
                psp[0].ap()[0:96, 0:TCS],
                wx_sb.ap()[:, k * 96:(k + 1) * 96],
                xs_c.ap()[:, k * TCS:(k + 1) * TCS],
                start=(k == 0), stop=(k == 15)),
                reads=[wx_sb, xs_c], writes=[psp[0]])
        em.op("v", lambda v: v.tensor_scalar(xp_sb.ap()[0:96, 0:TCS], psp[0].ap()[0:96, 0:TCS],
                                             bx_sb.ap(), None, OP.add),
              reads=[psp[0], bx_sb], writes=[xp_sb])
        em.op("v", lambda v: v.tensor_copy(bc16.ap()[0:96, 0:TCS], xp_sb.ap()[0:96, 0:TCS]),
              reads=[xp_sb], writes=[bc16])
        em.op("g", lambda g: g.dma_start(out=bc_dram, in_=bc16.ap()[64:96, 0:TCS]),
              reads=[bc16], writes=[bc_dram], dma=True)
        em.op("g", lambda g: g.dma_start(
            out=bcb.ap().rearrange("p (n t) -> p n t", t=TCS),
            in_=bc_dram[0:16, :].unsqueeze(0).broadcast_to([128, 16, TCS])),
            reads=[bc_dram], writes=[bcb], dma=True)
        em.op("g", lambda g: g.dma_start(
            out=bcc.ap().rearrange("p (n t) -> p n t", t=TCS),
            in_=bc_dram[16:32, :].unsqueeze(0).broadcast_to([128, 16, TCS])),
            reads=[bc_dram], writes=[bcc], dma=True)
        fence("g", [bcb, bcc], bcc.ap()[:, 0:16], BF16)
        def delta_chain(dcp):
            dp_ = dcp % 2
            pw = psp[1 + (dcp % 3)]
            em.op("t", lambda t, dcp=dcp, pw=pw: t.matmul(
                pw.ap()[:, 0:TCS],
                wdt_sb.ap()[:, dcp * 128:(dcp + 1) * 128],
                bc16.ap()[0:64, 0:TCS],
                start=True, stop=True),
                reads=[wdt_sb, bc16], writes=[pw])
            em.op("v", lambda v, dcp=dcp, dp_=dp_, pw=pw: v.tensor_scalar(
                sp_t1[dp_].ap(), pw.ap()[:, 0:TCS], bdt_sb.ap()[:, dcp:dcp + 1], None, OP.add),
                reads=[pw, bdt_sb], writes=[sp_t1[dp_]])
            em.op("v", lambda v, dp_=dp_: v.scalar_tensor_tensor(
                sp_t2[dp_].ap(), sp_t1[dp_].ap(), -1.0, sp_t1[dp_].ap(), OP.mult, OP.min),
                reads=[sp_t1[dp_]], writes=[sp_t2[dp_]])
            em.op("s", lambda s, dp_=dp_: s.activation(sp_t2[dp_].ap(), sp_t2[dp_].ap(), AF.Exp),
                  reads=[sp_t2[dp_]], writes=[sp_t2[dp_]])
            em.op("s", lambda s, dp_=dp_: s.activation(sp_t2[dp_].ap(), sp_t2[dp_].ap(), AF.Ln,
                                                       bias=ones_col.ap()),
                  reads=[sp_t2[dp_], ones_col], writes=[sp_t2[dp_]])
            em.op("v", lambda v, dp_=dp_: v.scalar_tensor_tensor(
                delta_t[dp_].ap(), sp_t1[dp_].ap(), 0.0, sp_t2[dp_].ap(), OP.max, OP.add),
                reads=[sp_t1[dp_], sp_t2[dp_]], writes=[delta_t[dp_]])
            em.op("v", lambda v, dcp=dcp, dp_=dp_: v.tensor_tensor(
                u_t[dp_].ap(), delta_t[dp_].ap(), xs_c.ap()[:, dcp * TCS:(dcp + 1) * TCS],
                OP.mult),
                reads=[delta_t[dp_], xs_c], writes=[u_t[dp_]])

        delta_chain(0)
        for dcp in range(8):
            dp_ = dcp % 2
            if dcp + 1 < 8:
                delta_chain(dcp + 1)
            for n in range(NST):
                pp = n % 2
                col = dcp * 16 + n
                em.op("s", lambda s, col=col, pp=pp, dp_=dp_: s.activation(
                    da2[pp].ap(), delta_t[dp_].ap(), AF.Exp,
                    scale=a_sb.ap()[:, col:col + 1]),
                    reads=[delta_t[dp_], a_sb], writes=[da2[pp]])
                em.op("v", lambda v, n=n, pp=pp, dp_=dp_: v.tensor_tensor(
                    w2[pp].ap(), u_t[dp_].ap(),
                    bcb.ap()[:, n * TCS:(n + 1) * TCS], OP.mult),
                    reads=[u_t[dp_], bcb], writes=[w2[pp]])
                em.op("v", lambda v, col=col, pp=pp: v.tensor_tensor_scan(
                    h2[pp].ap(), da2[pp].ap(), w2[pp].ap(), state.ap()[:, col:col + 1],
                    OP.mult, OP.add),
                    reads=[da2[pp], w2[pp], state], writes=[h2[pp]])
                em.op("g", lambda g, col=col, pp=pp: g.tensor_copy(
                    state.ap()[:, col:col + 1], h2[pp].ap()[:, TCS - 1:TCS]),
                    reads=[h2[pp]], writes=[state])
                em.op("g", lambda g, n=n, pp=pp, dp_=dp_: g.tensor_tensor(
                    hp_all[dp_].ap()[:, n * TCS:(n + 1) * TCS], h2[pp].ap(),
                    bcc.ap()[:, n * TCS:(n + 1) * TCS], OP.mult),
                    reads=[h2[pp], bcc], writes=[hp_all[dp_]])
            hp = hp_all[dp_]
            em.op("v", lambda v, hp=hp: v.tensor_tensor(
                hp.ap()[:, 0:8 * TCS], hp.ap()[:, 0:8 * TCS], hp.ap()[:, 8 * TCS:16 * TCS],
                OP.add),
                reads=[hp], writes=[hp])
            em.op("v", lambda v, hp=hp: v.tensor_tensor(
                hp.ap()[:, 0:4 * TCS], hp.ap()[:, 0:4 * TCS], hp.ap()[:, 4 * TCS:8 * TCS],
                OP.add),
                reads=[hp], writes=[hp])
            em.op("v", lambda v, hp=hp: v.tensor_tensor(
                hp.ap()[:, 0:2 * TCS], hp.ap()[:, 0:2 * TCS], hp.ap()[:, 2 * TCS:4 * TCS],
                OP.add),
                reads=[hp], writes=[hp])
            em.op("v", lambda v, hp=hp: v.tensor_tensor(
                y_s.ap(), hp.ap()[:, 0:TCS], hp.ap()[:, TCS:2 * TCS], OP.add),
                reads=[hp], writes=[y_s])
            em.op("v", lambda v, dcp=dcp: v.scalar_tensor_tensor(
                yg_t.ap(), xs_c.ap()[:, dcp * TCS:(dcp + 1) * TCS],
                dp_sb.ap()[:, dcp:dcp + 1], y_s.ap(), OP.mult, OP.add),
                reads=[xs_c, dp_sb, y_s], writes=[yg_t])
            em.op("v", lambda v, dcp=dcp, dp_=dp_: v.tensor_tensor(
                yg2_t[dp_].ap(), yg_t.ap(), z_c.ap()[:, dcp * TCS:(dcp + 1) * TCS], OP.mult),
                reads=[yg_t, z_c], writes=[yg2_t[dp_]])
            yh = 0 if ci < NCH_S // 2 else 1
            ygt0 = gt0 % (TSEQ // 2)
            em.op("s", lambda s, dcp=dcp, ygt0=ygt0, dp_=dp_, yh=yh: s.dma_start(
                out=cyg_in[yh][dcp, :, ygt0:ygt0 + TCS], in_=yg2_t[dp_].ap()),
                reads=[yg2_t[dp_]], writes=[cyg_in[yh]], dma=True)
        if ci == NCH_S // 2 - 1:
            fence("s", [cyg_in[0]], cyg_in[0][0, 0:128, 0:16], BF16, sbuf_src=False)
            em.op("g", lambda g: _cc(g, "AllGather", [cyg_in[0]], [cyg_out[0]]),
                  reads=[cyg_in[0]], writes=[cyg_out[0]])
    fence("s", [cyg_in[1]], cyg_in[1][0, 0:128, 0:16], BF16, sbuf_src=False)
    em.op("g", lambda g: _cc(g, "AllGather", [cyg_in[1]], [cyg_out[1]]),
          reads=[cyg_in[1]], writes=[cyg_out[1]])
    em.barrier()

    ovl.reset()
    wout_sb = ovl.t("wout_sb", [128, 16 * D], BF16)
    yg2_lo = [ovl.t("yg2_lo", [128, 16 * TCP], BF16) for _ in range(2)]
    yg2_hi = [ovl.t("yg2_hi", [128, 16 * TCP], BF16) for _ in range(2)]
    yg2_e = [ovl.t("yg2_e", [128, 16 * TCP], BF16) for _ in range(2)]
    ymask = ovl.t("ymask", [128, 16 * TCP], mybir.dt.uint8)
    hcol = ovl.t("hcol", [128, 1], I32)
    res_e = [ovl.t("res_e", [128, (TCP // 128) * D], F32) for _ in range(2)]
    sqs = ovl.t("sqs", [128, D], F32)
    em.op("q", lambda q: q.dma_start(out=hcol.ap(), in_=din["hsel"].broadcast_to([128, 1])),
          writes=[hcol], dma=True)
    fence("q", [hcol], fsk_dr[F32][:, 0:16], F32, sbuf_src=False)
    em.op("v", lambda v: v.tensor_copy(ymask.ap(),
                                       hcol.ap().broadcast_to([128, 16 * TCP])),
          reads=[hcol], writes=[ymask])
    for k in range(16):
        em.op("q", lambda q, k=k: q.dma_start(
            out=wout_sb.ap()[:, k * D:(k + 1) * D],
            in_=wout_bf[k * 128:(k + 1) * 128, :]),
            reads=[wout_bf], writes=[wout_sb], dma=True)
    fence("q", [wout_sb], wout_sb.ap()[:, 0:16], BF16)
    for ci in range(NCH_P):
        t0 = ci * TCP
        ntt = TCP // 128
        cp = ci % 2
        em.op("q", lambda q, t0=t0, cp=cp: q.dma_start(
            out=yg2_lo[cp].ap().rearrange("p (k t) -> p k t", t=TCP),
            in_=cyg_out[0].rearrange("j k p t -> p (j k) t")[:, :, t0:t0 + TCP]),
            reads=[cyg_out[0]], writes=[yg2_lo[cp]], dma=True)
        em.op("q", lambda q, t0=t0, cp=cp: q.dma_start(
            out=yg2_hi[cp].ap().rearrange("p (k t) -> p k t", t=TCP),
            in_=cyg_out[1].rearrange("j k p t -> p (j k) t")[:, :, t0:t0 + TCP]),
            reads=[cyg_out[1]], writes=[yg2_hi[cp]], dma=True)
        fence("q", [yg2_lo[cp], yg2_hi[cp]], yg2_hi[cp].ap()[:, 0:16], BF16)
        em.op("v", lambda v, cp=cp: v.select(yg2_e[cp].ap(), ymask.ap(),
                                             yg2_hi[cp].ap(), yg2_lo[cp].ap()),
              reads=[ymask, yg2_hi[cp], yg2_lo[cp]], writes=[yg2_e[cp]])
        for tt in range(ntt):
            ti = (t0 // 128) + tt
            tp2 = tt % 2
            em.op("q", lambda q, ti=ti, tt=tt, cp=cp: q.dma_start(
                out=res_e[cp].ap()[:, tt * D:(tt + 1) * D],
                in_=xres[ti * 128:(ti + 1) * 128, :]),
                reads=[xres], writes=[res_e[cp]], dma=True)
            fence("q", [res_e[cp]], res_e[cp].ap()[:, tt * D:tt * D + 16], F32)
            for half in range(2):
                pd = psp[2 * tp2 + half]
                for k in range(16):
                    em.op("t", lambda t, k=k, half=half, tt=tt, cp=cp, pd=pd: t.matmul(
                        pd.ap(),
                        yg2_e[cp].ap()[:, k * TCP + tt * 128:k * TCP + (tt + 1) * 128],
                        wout_sb.ap()[:, k * D + half * 512:k * D + (half + 1) * 512],
                        start=(k == 0), stop=False),
                        reads=[yg2_e[cp], wout_sb], writes=[pd])
                em.op("t", lambda t, half=half, pd=pd: t.matmul(
                    pd.ap(),
                    ones_bf.ap(),
                    bout_bf.ap()[:, half * 512:(half + 1) * 512],
                    start=False, stop=True),
                    reads=[ones_bf, bout_bf], writes=[pd])
            for half in range(2):
                pd = psp[2 * tp2 + half]
                em.op("v", lambda v, tt=tt, half=half, cp=cp, pd=pd: v.tensor_tensor(
                    res_e[cp].ap()[:, tt * D + half * 512:tt * D + (half + 1) * 512],
                    res_e[cp].ap()[:, tt * D + half * 512:tt * D + (half + 1) * 512],
                    pd.ap(), OP.add),
                    reads=[pd, res_e[cp]], writes=[res_e[cp]])
            em.op("s", lambda s, ti=ti, tt=tt, cp=cp: s.dma_start(
                out=xres[ti * 128:(ti + 1) * 128, :],
                in_=res_e[cp].ap()[:, tt * D:(tt + 1) * D]),
                reads=[res_e[cp]], writes=[xres], dma=True)
            ln_stats(res_e[cp].ap()[:, tt * D:(tt + 1) * D], res_e[cp], ti, sqs.ap(), sqs)
    fence("s", [xres], xres[0:128, 0:16], F32, sbuf_src=False)
    ln_finalize()
    em.barrier()

    ffn_phase(xres, out_hbm)

    if DEBUG:
        dbg1 = nc.dram_tensor("dbg_xres", [TME, D], F32, kind="ExternalOutput").ap()
        dbg2 = nc.dram_tensor("dbg_cyg", [2, 2, 8, 128, TSEQ // 2], BF16, kind="ExternalOutput").ap()
        em.op("q", lambda q: q.dma_start(out=dbg1, in_=xres), reads=[xres], dma=True)
        for i in range(2):
            em.op("q", lambda q, i=i: q.dma_start(out=dbg2[i], in_=cyg_out[i]),
                  reads=[cyg_out[i]], dma=True)

    em.final_join()

    with nc.Block() as block:
        @block.sync
        def _(q):
            em.replay(q, "q")

        @block.vector
        def _(v):
            em.replay(v, "v")

        @block.scalar
        def _(s):
            em.replay(s, "s")

        @block.gpsimd
        def _(g):
            em.replay(g, "g")

        @block.tensor
        def _(t):
            em.replay(t, "t")

    lower_extended_insts(nc)
    return nc, stack


_PROG_CACHE = {}


def kernel(L0_attn_x, L0_attn_Win, L0_attn_bin, L0_attn_Wx, L0_attn_bx, L0_attn_Wdt,
           L0_attn_bdt, L0_attn_A_log, L0_attn_Dp, L0_attn_Wout, L0_attn_bout,
           L0_ffn_w_gate, L0_ffn_w_up, L0_ffn_w_down):
    x = np.asarray(L0_attn_x, np.float32)
    alog = np.asarray(L0_attn_A_log, np.float32)
    win = np.asarray(L0_attn_Win, np.float32)
    binv = np.asarray(L0_attn_bin, np.float32)
    wx = np.asarray(L0_attn_Wx, np.float32)
    wdt = np.asarray(L0_attn_Wdt, np.float32)
    bdt = np.asarray(L0_attn_bdt, np.float32)
    dp = np.asarray(L0_attn_Dp, np.float32)
    common = {
        "bx_t": np.ascontiguousarray(np.asarray(L0_attn_bx, np.float32).reshape(96, 1)),
        "Wout": np.ascontiguousarray(np.asarray(L0_attn_Wout, np.float32)),
        "bout_r": np.ascontiguousarray(np.asarray(L0_attn_bout, np.float32).reshape(1, 1024)),
        "w_gate": np.ascontiguousarray(np.asarray(L0_ffn_w_gate, np.float32)),
        "w_up": np.ascontiguousarray(np.asarray(L0_ffn_w_up, np.float32)),
        "w_down": np.ascontiguousarray(np.asarray(L0_ffn_w_down, np.float32)),
    }
    in_maps = []
    for c in range(8):
        b, h = c // 2, c % 2
        m = dict(common)
        m["x"] = np.ascontiguousarray(x[b, h * TME:(h + 1) * TME])
        lo, hi = h * 1024, (h + 1) * 1024
        olo, ohi = (1 - h) * 1024, (2 - h) * 1024
        m["Win"] = np.ascontiguousarray(np.concatenate(
            [win[:, lo:hi], win[:, olo:ohi], win[:, 2048 + lo:2048 + hi]], axis=1))
        binp = np.concatenate([binv[lo:hi], binv[olo:ohi], binv[2048 + lo:2048 + hi]])
        m["bin_t"] = np.ascontiguousarray(binp.reshape(24, 128).T)
        m["Wx"] = np.ascontiguousarray(np.concatenate([wx[lo:hi], wx[olo:ohi]], axis=0))
        m["Wdt"] = np.ascontiguousarray(wdt[:, lo:hi])
        m["bdt_t"] = np.ascontiguousarray(bdt[lo:hi].reshape(8, 128).T)
        m["dp_t"] = np.ascontiguousarray(dp[lo:hi].reshape(8, 128).T)
        m["alog_t"] = np.ascontiguousarray(
            alog[lo:hi].reshape(8, 128, 16).transpose(1, 0, 2).reshape(128, 128))
        m["hsel"] = np.full((1, 1), h, np.int32)
        in_maps.append(m)

    if "prog" not in _PROG_CACHE:
        _PROG_CACHE["prog"] = build_program()
    nc, _stack = _PROG_CACHE["prog"]
    res = run_bass_kernel_spmd(nc, in_maps, core_ids=list(range(8)))
    out = np.empty((4, TSEQ, D), np.float32)
    for c in range(8):
        b, h = c // 2, c % 2
        out[b, h * TME:(h + 1) * TME] = np.asarray(res.results[c]["out"], np.float32)
    if DEBUG:
        global DBG_RES
        DBG_RES = res
    return out
